# version 03
# baseline (speedup 1.0000x reference)
import numpy as np
import concourse.bass as bass
import concourse.mybir as mybir
from concourse.bass_utils import run_bass_kernel_spmd

F32 = mybir.dt.float32
BF16 = mybir.dt.bfloat16
AF = mybir.ActivationFunctionType
ALU = mybir.AluOpType
AX = mybir.AxisListType.X

L = 8192
D = 1024
DI = 2048
DFF = 4096
NE = 8
DS = 16
NCH = 8
C = 1024
NSUB = 4
SUB = 256
NDT = 8
NG = 16
EPS = 1e-5
XRS = 260
YO = 4160


def build_nc(nch=NCH):
    nc = bass.Bass("TRN2", target_bir_lowering=False)

    d_x = nc.dram_tensor("x", [L, D], F32, kind="ExternalInput").ap()
    d_w1 = nc.dram_tensor("w1s", [NE, 4, 8, 128, 1024], F32, kind="ExternalInput").ap()
    d_w2 = nc.dram_tensor("w2s", [NE, 4, 8, 128, 1024], F32, kind="ExternalInput").ap()
    d_win = nc.dram_tensor("wins", [4, 8, 128, 1024], F32, kind="ExternalInput").ap()
    d_wout = nc.dram_tensor("wouts", [2, 16, 128, 512], F32, kind="ExternalInput").ap()
    d_wxdt = nc.dram_tensor("wxdt", [128, 3584], F32, kind="ExternalInput").ap()
    d_gate = nc.dram_tensor("gates", [128, 64], F32, kind="ExternalInput").ap()
    d_b2 = nc.dram_tensor("b2s", [8, 1024], F32, kind="ExternalInput").ap()
    d_b1 = nc.dram_tensor("b1s", [128, 256], F32, kind="ExternalInput").ap()
    d_cw = nc.dram_tensor("convw", [128, 64], F32, kind="ExternalInput").ap()
    d_cb = nc.dram_tensor("convb", [128, 16], F32, kind="ExternalInput").ap()
    d_bdt = nc.dram_tensor("bdts", [128, 16], F32, kind="ExternalInput").ap()
    d_D = nc.dram_tensor("Dg", [128, 16], F32, kind="ExternalInput").ap()
    d_alog = nc.dram_tensor("alog", [128, 256], F32, kind="ExternalInput").ap()
    d_id = nc.dram_tensor("ident", [128, 128], F32, kind="ExternalInput").ap()
    d_oc = nc.dram_tensor("onesc", [128, 1], F32, kind="ExternalInput").ap()
    d_or = nc.dram_tensor("onesr", [1, 128], F32, kind="ExternalInput").ap()
    d_sel = nc.dram_tensor("sel16", [16, 2048], F32, kind="ExternalInput").ap()
    d_out = nc.dram_tensor("out", [L, D], F32, kind="ExternalOutput").ap()

    s_w1 = nc.dram_tensor("w1b", [NE, 4, 8, 128, 1024], BF16, kind="Internal").ap()
    s_w2 = nc.dram_tensor("w2b", [NE, 4, 8, 128, 1024], BF16, kind="Internal").ap()
    s_win = nc.dram_tensor("winb", [4, 8, 128, 1024], BF16, kind="Internal").ap()
    s_wout = nc.dram_tensor("woutb", [2, 16, 128, 512], BF16, kind="Internal").ap()

    resid = nc.alloc_sbuf_tensor("resid", [128, 8192], F32)
    xnbf = nc.alloc_sbuf_tensor("xnbf", [128, 8192], BF16)
    slots = nc.alloc_sbuf_tensor("slots", [128, 8192], F32)
    slotb = slots[:].bitcast(BF16)
    ar1 = nc.alloc_sbuf_tensor("ar1", [128, 8256], F32)
    ar2 = nc.alloc_sbuf_tensor("ar2", [128, 4096], F32)
    ar2b = ar2[:].bitcast(BF16)
    ovl = nc.alloc_sbuf_tensor("ovl", [128, 13312], F32)
    OV = ovl[:]
    OVB = ovl[:].bitcast(BF16)
    zbuf = OV[:, 0:4096]
    dtb = OV[:, 4096:8192]
    sa_sb = OV[:, 8192:8704]
    sh_sb = OV[:, 8704:9216]
    st1_sb = [OV[:, 9216:9472], OV[:, 9472:9728], OV[:, 13056:13312]]
    sb1_sb = [OV[:, 9728:9984], OV[:, 12544:12800]]
    sp1_sb = OV[:, 9984:10496]
    sp2_sb = [OV[:, 10496:10752], OV[:, 12800:13056]]
    cv_sb = OV[:, 10752:11264]
    cc_sb = OV[:, 11264:11776]
    bc_sb = OV[0:16, 11776:12288]
    dl_sb = OVB[0:64, 24576:24832]
    bco_sb = OV[0:1, 11264:11776]
    mt_sb = OV[:, 0:1024]
    ln_sb = OV[0:1, 1024:3072]
    rp_sb = OV[:, 3072:3104]
    rmax = OV[:, 3104:3116]
    mw_sb = OV[:, 3116:3148]
    mwt_sb = OV[0:8, 3148:4172]

    wx_sb = nc.alloc_sbuf_tensor("wx_sb", [128, 1536], BF16)
    wdt_sb = nc.alloc_sbuf_tensor("wdt_sb", [64, 2048], BF16)
    gate_sb = nc.alloc_sbuf_tensor("gate_sb", [128, 64], F32)
    b2_sb = nc.alloc_sbuf_tensor("b2_sb", [8, 1024], F32)
    b1_sb = nc.alloc_sbuf_tensor("b1_sb", [128, 256], F32)
    cw_sb = nc.alloc_sbuf_tensor("cw_sb", [128, 64], F32)
    cb_sb = nc.alloc_sbuf_tensor("cb_sb", [128, 16], F32)
    bdt_sb = nc.alloc_sbuf_tensor("bdt_sb", [128, 16], F32)
    D_sb = nc.alloc_sbuf_tensor("D_sb", [128, 16], F32)
    A_sb = nc.alloc_sbuf_tensor("A_sb", [128, 256], F32)
    id_sb = nc.alloc_sbuf_tensor("id_sb", [128, 128], F32)
    oc_sb = nc.alloc_sbuf_tensor("oc_sb", [128, 1], F32)
    or_sb = nc.alloc_sbuf_tensor("or_sb", [1, 128], F32)
    sel_sb = nc.alloc_sbuf_tensor("sel_sb", [16, 2048], F32)
    hst_sb = nc.alloc_sbuf_tensor("hst_sb", [128, 256], F32)
    halo_sb = nc.alloc_sbuf_tensor("halo_sb", [128, 48], F32)

    pb = [nc.alloc_psum_tensor(f"pb{i}", [128, 512], F32) for i in range(8)]

    TE, SE, VE, GP, SY = nc.tensor, nc.scalar, nc.vector, nc.gpsimd, nc.sync

    class Ctr:
        __slots__ = ("h", "n")
        def __init__(self, name):
            self.h = nc.alloc_semaphore(name)
            self.n = 0
        def inc(self, instr, k=1):
            instr.then_inc(self.h, k)
            self.n += k
            return self.n

    cCONST = Ctr("cCONST"); cPR = Ctr("cPR")
    semW = [Ctr(f"semW{i}") for i in range(4)]
    semI = [Ctr(f"semI{i}") for i in range(2)]
    semO = [Ctr(f"semO{i}") for i in range(2)]
    cWF = Ctr("cWF")
    cTE = Ctr("cTE"); cSE = Ctr("cSE"); cVE = Ctr("cVE"); cGP = Ctr("cGP")

    waited = {}
    def wait(eng, ctr, n):
        if n is None or n <= 0:
            return
        key = (id(eng), id(ctr))
        if waited.get(key, -1) >= n:
            return
        waited[key] = n
        eng.wait_ge(ctr.h, n)

    bank_rd = [[] for _ in range(8)]
    def te_take(bi):
        for ctr, n in bank_rd[bi]:
            wait(TE, ctr, n)
        bank_rd[bi] = []
    def read_bank(bi, ctr, n):
        bank_rd[bi].append((ctr, n))

    st = {"piece": 0, "ht": 0, "yt": 0, "p01": 0, "out_n": 0,
          "mt_slot": [None, None],
          "stage_in": [None, None],
          "stage_out": [None, None],
          }

    def load_piece(src3d, kt, m):
        i = st["piece"]; st["piece"] += 1
        off = (i % 2) * 8192
        wait(SY, cWF, i - 1)
        dst = slotb[:, off:off + kt * m].rearrange("p (a b) -> p a b", a=kt, b=m)
        sw = semW[i % 4]
        SY.dma_start(dst, src3d.rearrange("a p b -> p a b")).then_inc(sw.h, 16)
        sw.n += 16
        return i

    def wait_piece(eng, i):
        wait(eng, semW[i % 4], 16 * (i // 4 + 1))

    def piece_view(i, kt, m):
        off = (i % 2) * 8192
        return slotb[:, off:off + kt * m].rearrange("p (a b) -> p a b", a=kt, b=m)

    with nc.Block():
        for dst, src in [(gate_sb, d_gate), (b2_sb, d_b2), (b1_sb, d_b1),
                         (cw_sb, d_cw), (cb_sb, d_cb), (bdt_sb, d_bdt),
                         (D_sb, d_D), (A_sb, d_alog), (id_sb, d_id),
                         (oc_sb, d_oc), (or_sb, d_or), (sel_sb, d_sel)]:
            SY.dma_start(dst[:], src).then_inc(cCONST.h, 16); cCONST.n += 16
        GP.dma_start(wx_sb[:], d_wxdt[:, 0:1536]).then_inc(cPR.h, 16); cPR.n += 16
        GP.dma_start(wdt_sb[:], d_wxdt[0:64, 1536:3584]).then_inc(cPR.h, 16); cPR.n += 16
        for e in range(NE):
            for q in range(4):
                GP.dma_start(s_w1[e, q], d_w1[e, q]).then_inc(cPR.h, 16); cPR.n += 16
                GP.dma_start(s_w2[e, q], d_w2[e, q]).then_inc(cPR.h, 16); cPR.n += 16
        for q in range(4):
            GP.dma_start(s_win[q], d_win[q]).then_inc(cPR.h, 16); cPR.n += 16
        for h in range(2):
            GP.dma_start(s_wout[h], d_wout[h]).then_inc(cPR.h, 16); cPR.n += 16
        prep_total = cPR.n

        wait(SE, cCONST, cCONST.n)
        wait(TE, cCONST, cCONST.n)
        wait(VE, cCONST, cCONST.n)
        wait(GP, cCONST, cCONST.n)
        i = SE.activation(sp1_sb[:, 0:256], A_sb[:], AF.Exp)
        m = cSE.inc(i)
        wait(VE, cSE, m)
        VE.tensor_scalar(A_sb[:], sp1_sb[:, 0:256], -1.0, None, ALU.mult)
        VE.memset(halo_sb[:], 0.0)
        VE.memset(hst_sb[:], 0.0)
        wait(SY, cPR, prep_total)

        def emit_ln(sub, with_router, out_f32):
            t0 = sub * 512
            wait(SE, cTE, cTE.n); wait(SE, cVE, cVE.n); wait(SE, cGP, cGP.n)
            SE.drain(); VE.drain()
            for dt_ in range(NDT):
                i = SE.activation(ar2[:, dt_ * 512:(dt_ + 1) * 512],
                                  resid[:, dt_ * 1024 + t0:dt_ * 1024 + t0 + 512],
                                  AF.Square)
            mse = cSE.inc(i)
            te_take(6); te_take(7)
            wait(TE, cSE, mse)
            for dt_ in range(NDT):
                TE.matmul(pb[6][0:1, 0:512], oc_sb[:],
                          resid[:, dt_ * 1024 + t0:dt_ * 1024 + t0 + 512],
                          start=(dt_ == 0), stop=(dt_ == NDT - 1))
            for dt_ in range(NDT):
                i = TE.matmul(pb[7][0:1, 0:512], oc_sb[:],
                              ar2[:, dt_ * 512:(dt_ + 1) * 512],
                              start=(dt_ == 0), stop=(dt_ == NDT - 1))
            mte = cTE.inc(i)
            wait(VE, cTE, mte)
            VE.tensor_scalar(ln_sb[0:1, 1536:2048], pb[7][0:1, 0:512],
                             1.0 / D, None, ALU.mult)
            i = VE.tensor_scalar(ln_sb[0:1, 1024:1536], pb[6][0:1, 0:512],
                                 1.0 / D, None, ALU.mult)
            mve = cVE.inc(i)
            read_bank(6, cVE, mve); read_bank(7, cVE, mve)
            wait(SE, cVE, mve)
            i = SE.activation(ln_sb[0:1, 0:512], ln_sb[0:1, 1024:1536], AF.Square)
            mse = cSE.inc(i)
            wait(VE, cSE, mse)
            VE.drain()
            i = VE.scalar_tensor_tensor(ln_sb[0:1, 512:1024], ln_sb[0:1, 1536:2048],
                                        EPS, ln_sb[0:1, 0:512],
                                        ALU.add, ALU.subtract)
            mve = cVE.inc(i)
            wait(SE, cVE, mve)
            SE.activation(ln_sb[0:1, 1536:2048], ln_sb[0:1, 512:1024], AF.Ln)
            SE.drain()
            i = SE.activation(ln_sb[0:1, 0:512], ln_sb[0:1, 1536:2048],
                              AF.Exp, scale=-0.5)
            mse = cSE.inc(i)
            wait(VE, cSE, mse)
            VE.drain()
            i = VE.scalar_tensor_tensor(bco_sb, ln_sb[0:1, 1024:1536],
                                        -1.0, ln_sb[0:1, 0:512], ALU.mult, ALU.mult)
            mve = cVE.inc(i)
            te_take(4); te_take(5)
            wait(TE, cVE, mve)
            TE.matmul(pb[4][:, 0:512], or_sb[:], ln_sb[0:1, 0:512], start=True, stop=True)
            i = TE.matmul(pb[5][:, 0:512], or_sb[:], bco_sb,
                          start=True, stop=True)
            mte = cTE.inc(i)
            wait(VE, cTE, mte)
            for dt_ in range(NDT):
                VE.tensor_tensor(ar2[:, dt_ * 512:(dt_ + 1) * 512],
                                 resid[:, dt_ * 1024 + t0:dt_ * 1024 + t0 + 512],
                                 pb[4][:, 0:512], ALU.mult)
            VE.drain()
            for dt_ in range(NDT):
                dst = (ar1[:, dt_ * 512:(dt_ + 1) * 512] if out_f32
                       else xnbf[:, dt_ * 1024 + t0:dt_ * 1024 + t0 + 512])
                i = VE.tensor_tensor(dst, ar2[:, dt_ * 512:(dt_ + 1) * 512],
                                     pb[5][:, 0:512], ALU.add)
            mve = cVE.inc(i)
            read_bank(4, cVE, mve); read_bank(5, cVE, mve)
            if out_f32:
                wait(SE, cVE, mve)
                for dt_ in range(NDT):
                    i = SE.activation(xnbf[:, dt_ * 1024 + t0:dt_ * 1024 + t0 + 512],
                                      ar1[:, dt_ * 512:(dt_ + 1) * 512], AF.Identity)
                cSE.inc(i)
            if with_router:
                emit_router(sub, mve)

        def emit_router(sub, m_xn):
            t0 = sub * 512
            te_take(6)
            wait(TE, cVE, m_xn)
            for tt in range(4):
                for kt in range(NDT):
                    i = TE.matmul(pb[6][:, tt * 8:tt * 8 + 8],
                                  ar1[:, kt * 512 + tt * 128:kt * 512 + tt * 128 + 128],
                                  gate_sb[:, kt * 8:kt * 8 + 8],
                                  start=(kt == 0), stop=(kt == NDT - 1))
            mte = cTE.inc(i)
            wait(VE, cTE, mte)
            for tt in range(4):
                VE.reduce_max(rmax[:, tt:tt + 1], pb[6][:, tt * 8:tt * 8 + 8], axis=AX)
            VE.drain()
            for tt in range(4):
                i = VE.tensor_scalar(rmax[:, 4 + tt:5 + tt], rmax[:, tt:tt + 1],
                                     -1.0, None, ALU.mult)
            mve = cVE.inc(i)
            wait(SE, cVE, mve)
            for tt in range(4):
                i = SE.activation(rp_sb[:, tt * 8:tt * 8 + 8],
                                  pb[6][:, tt * 8:tt * 8 + 8],
                                  AF.Exp, bias=rmax[:, 4 + tt:5 + tt])
            mse = cSE.inc(i)
            wait(VE, cSE, mse)
            for tt in range(4):
                VE.reduce_sum(rmax[:, 8 + tt:9 + tt], rp_sb[:, tt * 8:tt * 8 + 8], axis=AX)
            VE.drain()
            for tt in range(4):
                VE.reciprocal(rmax[:, 8 + tt:9 + tt], rmax[:, 8 + tt:9 + tt])
            VE.drain()
            for tt in range(4):
                VE.tensor_scalar(rmax[:, 8 + tt:9 + tt], rmax[:, 8 + tt:9 + tt],
                                 0.5, None, ALU.mult)
            VE.drain()
            for tt in range(4):
                i = VE.tensor_scalar(mw_sb[:, tt * 8:tt * 8 + 8],
                                     pb[6][:, tt * 8:tt * 8 + 8],
                                     rmax[:, tt:tt + 1], rmax[:, 8 + tt:9 + tt],
                                     ALU.is_equal, ALU.mult)
            mve = cVE.inc(i)
            read_bank(6, cVE, mve)
            te_take(7)
            wait(TE, cVE, mve)
            for tt in range(4):
                i = TE.transpose(pb[7][0:8, tt * 128:(tt + 1) * 128],
                                 mw_sb[:, tt * 8:tt * 8 + 8], id_sb[:])
            mte = cTE.inc(i)
            wait(SE, cTE, mte)
            i = SE.activation(mwt_sb[0:8, t0:t0 + 512], pb[7][0:8, 0:512], AF.Identity)
            mse = cSE.inc(i)
            read_bank(7, cSE, mse)

        def emit_ffn(ch):
            m_mwt = cSE.n
            for j in range(NDT):
                for nh in range(2):
                    bi = 2 + (st["yt"] % 2); st["yt"] += 1
                    te_take(bi)
                    wait(TE, cSE, m_mwt)
                    i = TE.matmul(pb[bi][:, 0:512], b2_sb[0:8, j * 128:(j + 1) * 128],
                                  mwt_sb[0:8, nh * 512:nh * 512 + 512],
                                  start=True, stop=True)
                    mte = cTE.inc(i)
                    wait(VE, cTE, mte)
                    rs = resid[:, j * 1024 + nh * 512:j * 1024 + nh * 512 + 512]
                    i = VE.tensor_tensor(rs, pb[bi][:, 0:512], rs, ALU.add)
                    mve = cVE.inc(i)
                    read_bank(bi, cVE, mve)
            for e in range(NE):
                for nh in range(2):
                    bi = 4 + nh
                    te_take(bi)
                    i = TE.matmul(pb[bi][:, 0:512], sel_sb[0:8, e * 128:(e + 1) * 128],
                                  mwt_sb[0:8, nh * 512:nh * 512 + 512],
                                  start=True, stop=True)
                m_mbc = cTE.inc(i)
                for q in range(4):
                    p1 = load_piece(s_w1[e, q], 8, 1024)
                    p2 = load_piece(s_w2[e, q], 8, 1024)
                    w1v = piece_view(p1, 8, 1024)
                    w2v = piece_view(p2, 8, 1024)
                    wait_piece(TE, p1)
                    st["need_se_drain"] = True
                    for j in range(8):
                        for nh in range(2):
                            bi = st["ht"] % 2; st["ht"] += 1
                            te_take(bi)
                            for kt in range(8):
                                i = TE.matmul(
                                    pb[bi][:, 0:512],
                                    w1v[:, kt, j * 128:(j + 1) * 128],
                                    xnbf[:, kt * 1024 + nh * 512:kt * 1024 + nh * 512 + 512],
                                    start=(kt == 0), stop=(kt == 7))
                            mte = cTE.inc(i)
                            if j == 7 and nh == 1:
                                cWF.inc(TE.drain())
                            wait(SE, cTE, mte)
                            if st.pop("need_se_drain", False):
                                SE.drain()
                            bcol = e * 32 + q * 8 + j
                            i = SE.activation(
                                ar2b[:, j * 1024 + nh * 512:j * 1024 + nh * 512 + 512],
                                pb[bi][:, 0:512], AF.Gelu,
                                bias=b1_sb[:, bcol:bcol + 1])
                            mse = cSE.inc(i)
                            read_bank(bi, cSE, mse)
                    m_hq = cSE.n
                    wait_piece(TE, p2)
                    wait(TE, cSE, m_hq)
                    st["need_ve_drain"] = True
                    for j in range(8):
                        for nh in range(2):
                            bi = 2 + (st["yt"] % 2); st["yt"] += 1
                            te_take(bi)
                            for kt in range(8):
                                i = TE.matmul(
                                    pb[bi][:, 0:512],
                                    w2v[:, kt, j * 128:(j + 1) * 128],
                                    ar2b[:, kt * 1024 + nh * 512:kt * 1024 + nh * 512 + 512],
                                    start=(kt == 0), stop=(kt == 7))
                            mte = cTE.inc(i)
                            if j == 7 and nh == 1:
                                cWF.inc(TE.drain())
                            wait(VE, cTE, mte)
                            if st.pop("need_ve_drain", False):
                                VE.drain()
                            ys = ar1[:, j * 1024 + nh * 512:j * 1024 + nh * 512 + 512]
                            if q == 0:
                                i = VE.tensor_copy(ys, pb[bi][:, 0:512])
                            else:
                                i = VE.tensor_tensor(ys, pb[bi][:, 0:512], ys, ALU.add)
                            mve = cVE.inc(i)
                            read_bank(bi, cVE, mve)
                wait(VE, cTE, m_mbc)
                VE.drain()
                t = 0
                for j in range(NDT):
                    for nh in range(2):
                        sl = (t % 2) * 512
                        wait(VE, cGP, st["mt_slot"][t % 2])
                        VE.drain()
                        i = VE.tensor_tensor(
                            mt_sb[:, sl:sl + 512],
                            ar1[:, j * 1024 + nh * 512:j * 1024 + nh * 512 + 512],
                            pb[4 + nh][:, 0:512], ALU.mult)
                        mve = cVE.inc(i)
                        wait(GP, cVE, mve)
                        rs = resid[:, j * 1024 + nh * 512:j * 1024 + nh * 512 + 512]
                        i = GP.tensor_tensor(rs, mt_sb[:, sl:sl + 512], rs, ALU.add)
                        st["mt_slot"][t % 2] = cGP.inc(i)
                        t += 1
                read_bank(4, cVE, cVE.n)
                read_bank(5, cVE, cVE.n)

        def xr(g, a, b):
            return ar1[:, g * XRS + a:g * XRS + b]
        def ya(g):
            return ar1[:, YO + g * 256:YO + (g + 1) * 256]
        def xbf(g):
            return ar2b[:, g * 256:(g + 1) * 256]
        def ybf(g):
            return ar2b[:, 4096 + g * 256:4096 + (g + 1) * 256]

        def emit_mamba_sub(ch, sub):
            t0 = sub * 256
            st["need_se_drain"] = True
            st["need_gp_drain"] = True
            for q in range(4):
                p = load_piece(s_win[q], 8, 1024)
                wv = piece_view(p, 8, 1024)
                wait_piece(TE, p)
                for mt in range(8):
                    bi = st["p01"] % 2; st["p01"] += 1
                    te_take(bi)
                    for kt in range(8):
                        i = TE.matmul(pb[bi][:, 0:256],
                                      wv[:, kt, mt * 128:(mt + 1) * 128],
                                      xnbf[:, kt * 1024 + t0:kt * 1024 + t0 + 256],
                                      start=(kt == 0), stop=(kt == 7))
                    mte = cTE.inc(i)
                    if mt == 7:
                        cWF.inc(TE.drain())
                    wait(SE, cTE, mte)
                    if st.pop("need_se_drain", False):
                        SE.drain()
                    gi = q * 8 + mt
                    if q < 2:
                        i = SE.activation(xr(gi, 3, 259), pb[bi][:, 0:256], AF.Identity)
                    else:
                        zi = gi - 16
                        i = SE.activation(zbuf[:, zi * 256:(zi + 1) * 256],
                                          pb[bi][:, 0:256], AF.Identity)
                    mse = cSE.inc(i)
                    read_bank(bi, cSE, mse)
                if q == 1:
                    st["m_xz_x"] = cSE.n
            m_xz = cSE.n
            wait(VE, cSE, st["m_xz_x"])
            xr_g = ar1[:, 0:NG * XRS].rearrange("p (g a) -> p g a", g=NG, a=XRS)
            halo_g = halo_sb[:].rearrange("p (g a) -> p g a", g=NG, a=3)
            VE.drain()
            VE.tensor_copy(xr_g[:, :, 0:3], halo_g)
            VE.drain()
            VE.tensor_copy(halo_g, xr_g[:, :, 256:259])
            VE.drain()
            silu_ms = {}
            for g0 in range(0, NG, 2):
                if g0 >= 2:
                    wait(VE, cSE, silu_ms[g0 - 1])
                    VE.drain()
                for g in (g0, g0 + 1):
                    sl = (g % 2) * 256
                    VE.tensor_scalar(cv_sb[:, sl:sl + 256], xr(g, 0, 256),
                                     cw_sb[:, g * 4:g * 4 + 1], None, ALU.mult)
                for k in range(1, 4):
                    VE.drain()
                    for g in (g0, g0 + 1):
                        sl = (g % 2) * 256
                        i = VE.scalar_tensor_tensor(
                            cv_sb[:, sl:sl + 256], xr(g, k, 256 + k),
                            cw_sb[:, g * 4 + k:g * 4 + k + 1],
                            cv_sb[:, sl:sl + 256], ALU.mult, ALU.add)
                mve = cVE.inc(i)
                wait(SE, cVE, mve)
                for g in (g0, g0 + 1):
                    sl = (g % 2) * 256
                    i = SE.activation(xr(g, 3, 259), cv_sb[:, sl:sl + 256], AF.Silu,
                                      bias=cb_sb[:, g:g + 1])
                    silu_ms[g] = cSE.inc(i)
                wait(VE, cSE, silu_ms[g0 + 1])
                for g in (g0, g0 + 1):
                    i = VE.tensor_copy(xbf(g), xr(g, 3, 259))
                cVE.inc(i)
            m_xbf = cVE.n
            te_take(2); te_take(3)
            wait(TE, cVE, m_xbf)
            for kt in range(NG):
                TE.matmul(pb[2][0:64, 0:256], wx_sb[:, kt * 96:kt * 96 + 64],
                          xbf(kt), start=(kt == 0), stop=(kt == NG - 1))
            for kt in range(NG):
                TE.matmul(pb[3][0:16, 0:256], wx_sb[:, kt * 96 + 64:kt * 96 + 80],
                          xbf(kt), start=(kt == 0), stop=(kt == NG - 1))
            for kt in range(NG):
                i = TE.matmul(pb[3][0:16, 256:512], wx_sb[:, kt * 96 + 80:kt * 96 + 96],
                              xbf(kt), start=(kt == 0), stop=(kt == NG - 1))
            mte = cTE.inc(i)
            wait(SE, cTE, mte)
            SE.activation(dl_sb[:], pb[2][0:64, 0:256], AF.Identity)
            i = SE.activation(bc_sb[0:16, 0:512], pb[3][0:16, 0:512], AF.Identity)
            m_bcs = cSE.inc(i)
            read_bank(2, cSE, m_bcs); read_bank(3, cSE, m_bcs)
            dtv_ms = {}
            for mt0 in range(0, NG, 2):
                bis = []
                for mt in (mt0, mt0 + 1):
                    bi = st["p01"] % 2; st["p01"] += 1
                    te_take(bi)
                    wait(TE, cSE, m_bcs)
                    i = TE.matmul(pb[bi][0:128, 0:256],
                                  wdt_sb[:, mt * 128:(mt + 1) * 128],
                                  dl_sb[:], start=True, stop=True)
                    bis.append(bi)
                mte = cTE.inc(i)
                wait(SE, cTE, mte)
                if mt0 >= 2:
                    wait(SE, cVE, dtv_ms[mt0 - 1])
                SE.drain()
                for k, mt in enumerate((mt0, mt0 + 1)):
                    SE.activation(sp1_sb[:, k * 256:k * 256 + 256], pb[bis[k]][:, 0:256],
                                  AF.Abs, bias=bdt_sb[:, mt:mt + 1])
                SE.drain()
                for k in range(2):
                    SE.activation(sp2_sb[k][:, 0:256], sp1_sb[:, k * 256:k * 256 + 256],
                                  AF.Exp, scale=-1.0)
                SE.drain()
                for k in range(2):
                    SE.activation(sp1_sb[:, k * 256:k * 256 + 256], sp2_sb[k][:, 0:256],
                                  AF.Ln, bias=1.0)
                SE.drain()
                for k, mt in enumerate((mt0, mt0 + 1)):
                    i = SE.activation(dtb[:, mt * 256:(mt + 1) * 256],
                                      pb[bis[k]][:, 0:256],
                                      AF.Relu, bias=bdt_sb[:, mt:mt + 1])
                mse = cSE.inc(i)
                read_bank(bis[0], cSE, mse); read_bank(bis[1], cSE, mse)
                wait(VE, cSE, mse)
                for k, mt in enumerate((mt0, mt0 + 1)):
                    i = VE.tensor_tensor(dtb[:, mt * 256:(mt + 1) * 256],
                                         sp1_sb[:, k * 256:k * 256 + 256],
                                         dtb[:, mt * 256:(mt + 1) * 256], ALU.add)
                    dtv_ms[mt] = cVE.inc(i)
            VE.drain()
            for g0 in range(0, NG, 2):
                for g in (g0, g0 + 1):
                    VE.tensor_scalar(ya(g), xr(g, 3, 259), D_sb[:, g:g + 1],
                                     None, ALU.mult)
                VE.drain()
                for g in (g0, g0 + 1):
                    i = VE.tensor_tensor(xr(g, 3, 259), dtb[:, g * 256:(g + 1) * 256],
                                         xr(g, 3, 259), ALU.mult)
            cVE.inc(i)
            VE.drain()
            base_gp = cGP.n
            scan_ms = {}
            gp_ms = {}
            pend = []
            for s in range(DS):
                bB = 4 + (s % 2); bC = 6 + (s % 2)
                te_take(bB); te_take(bC)
                TE.matmul(pb[bB][:, 0:256], sel_sb[0:16, s * 128:(s + 1) * 128],
                          bc_sb[0:16, 0:256], start=True, stop=True)
                i = TE.matmul(pb[bC][:, 0:256], sel_sb[0:16, s * 128:(s + 1) * 128],
                              bc_sb[0:16, 256:512], start=True, stop=True)
                mte = cTE.inc(i)
                csl = (s % 2) * 256
                if s >= 2:
                    wait(SE, cGP, gp_ms[(s - 2) * NG + (NG - 1)])
                wait(SE, cTE, mte)
                i = SE.activation(cc_sb[:, csl:csl + 256], pb[bC][:, 0:256], AF.Identity)
                msec = cSE.inc(i)
                read_bank(bC, cSE, msec)
                for g in range(NG):
                    t = s * NG + g
                    asl = (g % 2) * 256
                    wait(SE, cVE, dtv_ms[g])
                    if t >= 2:
                        wait(SE, cVE, scan_ms[t - 2])
                    SE.drain()
                    i = SE.activation(sa_sb[:, asl:asl + 256],
                                      dtb[:, g * 256:(g + 1) * 256], AF.Exp,
                                      scale=A_sb[:, g * 16 + s:g * 16 + s + 1])
                    msea = cSE.inc(i)
                    wait(VE, cSE, msea)
                    wait(VE, cTE, mte)
                    ib = VE.tensor_tensor(sb1_sb[t % 2], xr(g, 3, 259),
                                          pb[bB][:, 0:256], ALU.mult)
                    mvb = cVE.inc(ib)
                    for lag in list(pend):
                        if lag[0] <= t - 2:
                            pend.remove(lag)
                            wait(VE, cGP, gp_ms[lag[0]])
                            i2 = VE.tensor_tensor(ya(lag[1]), st1_sb[lag[0] % 3],
                                                  ya(lag[1]), ALU.add)
                            cVE.inc(i2)
                    wait(VE, cVE, mvb)
                    hsl = (g % 2) * 256
                    if t >= 2:
                        wait(VE, cGP, gp_ms[t - 2])
                    i = VE.tensor_tensor_scan(sh_sb[:, hsl:hsl + 256],
                                              sa_sb[:, asl:asl + 256], sb1_sb[t % 2],
                                              hst_sb[:, g * 16 + s:g * 16 + s + 1],
                                              ALU.mult, ALU.add)
                    scan_ms[t] = cVE.inc(i)
                    wait(GP, cVE, scan_ms[t])
                    wait(GP, cSE, msec)
                    if st.pop("need_gp_drain", False):
                        GP.drain()
                    GP.tensor_tensor(st1_sb[t % 3], sh_sb[:, hsl:hsl + 256],
                                     cc_sb[:, csl:csl + 256], ALU.mult)
                    i = GP.tensor_copy(hst_sb[:, g * 16 + s:g * 16 + s + 1],
                                       sh_sb[:, hsl + 255:hsl + 256])
                    gp_ms[t] = cGP.inc(i)
                    pend.append((t, g))
                read_bank(bB, cVE, cVE.n)
            SE.drain()
            for g in range(NG):
                i = SE.activation(zbuf[:, g * 256:(g + 1) * 256],
                                  zbuf[:, g * 256:(g + 1) * 256], AF.Silu)
            st["m_gz"] = cSE.inc(i)
            for lag in pend:
                wait(VE, cGP, gp_ms[lag[0]])
                i = VE.tensor_tensor(ya(lag[1]), st1_sb[lag[0] % 3],
                                     ya(lag[1]), ALU.add)
                cVE.inc(i)
            m_scan = cVE.n
            wait(VE, cSE, st["m_gz"])
            VE.drain()
            for g in range(NG):
                i = VE.tensor_tensor(zbuf[:, g * 256:(g + 1) * 256], ya(g),
                                     zbuf[:, g * 256:(g + 1) * 256], ALU.mult)
                cVE.inc(i)
                VE.drain()
                if g >= 1:
                    i = VE.tensor_copy(ybf(g - 1), zbuf[:, (g - 1) * 256:g * 256])
                    cVE.inc(i)
            VE.drain()
            i = VE.tensor_copy(ybf(NG - 1), zbuf[:, (NG - 1) * 256:NG * 256])
            cVE.inc(i)
            m_gate = cVE.n
            for h in range(2):
                p = load_piece(s_wout[h], 16, 512)
                wv = piece_view(p, 16, 512)
                wait_piece(TE, p)
                wait(TE, cVE, m_gate)
                for mt in range(4):
                    bi = st["p01"] % 2; st["p01"] += 1
                    te_take(bi)
                    for kt in range(NG):
                        i = TE.matmul(pb[bi][:, 0:256],
                                      wv[:, kt, mt * 128:(mt + 1) * 128],
                                      ybf(kt), start=(kt == 0), stop=(kt == NG - 1))
                    mte = cTE.inc(i)
                    if mt == 3:
                        cWF.inc(TE.drain())
                    wait(VE, cTE, mte)
                    dt_ = h * 4 + mt
                    rs = resid[:, dt_ * 1024 + t0:dt_ * 1024 + t0 + 256]
                    i = VE.tensor_tensor(rs, pb[bi][:, 0:256], rs, ALU.add)
                    mve = cVE.inc(i)
                    read_bank(bi, cVE, mve)

        def emit_in(ch):
            wait(SY, semO[0], semO[0].n)
            wait(SY, semO[1], semO[1].n)
            for tt in range(8):
                sl = (tt % 2) * 1024
                g = ch * 8 + tt
                wait(SY, cSE, st["stage_in"][tt % 2])
                si = semI[g % 2]
                SY.dma_start(ar2[:, sl:sl + 1024],
                             d_x[g * 128:(g + 1) * 128, :]).then_inc(si.h, 16)
                si.n += 16
                wait(TE, si, si.n)
                for dt_ in range(NDT):
                    bi = st["p01"] % 2; st["p01"] += 1
                    te_take(bi)
                    i = TE.transpose(pb[bi][:, 0:128],
                                     ar2[:, sl + dt_ * 128:sl + (dt_ + 1) * 128],
                                     id_sb[:])
                    mte = cTE.inc(i)
                    wait(SE, cTE, mte)
                    if dt_ == 0:
                        SE.drain()
                    i = SE.activation(
                        resid[:, dt_ * 1024 + tt * 128:dt_ * 1024 + tt * 128 + 128],
                        pb[bi][:, 0:128], AF.Identity)
                    mse = cSE.inc(i)
                    read_bank(bi, cSE, mse)
                st["stage_in"][tt % 2] = mse

        def emit_out(ch):
            for tt in range(8):
                sl = (tt % 2) * 1024
                g = ch * 8 + tt
                so = semO[g % 2]
                for dt_ in range(NDT):
                    bi = st["p01"] % 2; st["p01"] += 1
                    te_take(bi)
                    i = TE.transpose(
                        pb[bi][:, 0:128],
                        resid[:, dt_ * 1024 + tt * 128:dt_ * 1024 + tt * 128 + 128],
                        id_sb[:])
                    mte = cTE.inc(i)
                    wait(SE, cTE, mte)
                    if dt_ == 0:
                        wait(SE, so, so.n)
                        SE.drain()
                    i = SE.activation(ar2[:, sl + dt_ * 128:sl + (dt_ + 1) * 128],
                                      pb[bi][:, 0:128], AF.Identity)
                    mse = cSE.inc(i)
                    read_bank(bi, cSE, mse)
                wait(SY, cSE, mse)
                SY.dma_start(d_out[g * 128:(g + 1) * 128, :],
                             ar2[:, sl:sl + 1024]).then_inc(so.h, 16)
                so.n += 16

        for ch in range(nch):
            emit_in(ch)
            emit_ln(0, True, True)
            emit_ln(1, True, True)
            wait(TE, cSE, cSE.n)
            emit_ffn(ch)
            emit_ln(0, False, False)
            emit_ln(1, False, False)
            wait(TE, cVE, cVE.n)
            for sub in range(NSUB):
                emit_mamba_sub(ch, sub)
            emit_ln(0, True, True)
            emit_ln(1, True, True)
            wait(TE, cSE, cSE.n)
            emit_ffn(ch)
            wait(TE, cGP, cGP.n)
            wait(SE, cGP, cGP.n)
            emit_out(ch)
        SY.wait_ge(semO[0].h, semO[0].n)
        SY.wait_ge(semO[1].h, semO[1].n)

    return nc


def _prearrange(inputs):
    w1 = inputs["L0_ffn_w1"]
    w2 = inputs["L0_ffn_w2"]
    win = inputs["L0_attn_W_in"]
    wout = inputs["L0_attn_W_out"]
    wx = inputs["L0_attn_W_x"]
    wdt = inputs["L0_attn_W_dt"]
    gate = inputs["L0_ffn_gate_w"]
    b1 = inputs["L0_ffn_b1"]
    b2 = inputs["L0_ffn_b2"]
    cw = inputs["L0_attn_conv_w"]
    cb = inputs["L0_attn_conv_b"]
    bdt = inputs["L0_attn_b_dt"]
    Dv = inputs["L0_attn_D"]
    alog = inputs["L0_attn_A_log"]

    d = {}
    d["w1s"] = np.ascontiguousarray(
        w1.reshape(8, 8, 128, 4, 1024).transpose(0, 3, 1, 2, 4))
    d["w2s"] = np.ascontiguousarray(w2.reshape(8, 4, 8, 128, 1024))
    d["wins"] = np.ascontiguousarray(
        win.reshape(8, 128, 4, 1024).transpose(2, 0, 1, 3))
    d["wouts"] = np.ascontiguousarray(
        wout.reshape(16, 128, 2, 512).transpose(2, 0, 1, 3))
    wxdt = np.zeros((128, 3584), dtype=np.float32)
    wxdt[:, 0:1536] = wx.reshape(16, 128, 96).transpose(1, 0, 2).reshape(128, 1536)
    wxdt[0:64, 1536:3584] = wdt
    d["wxdt"] = wxdt
    d["gates"] = np.ascontiguousarray(
        gate.reshape(8, 128, 8).transpose(1, 0, 2).reshape(128, 64))
    d["b2s"] = np.ascontiguousarray(b2)
    d["b1s"] = np.ascontiguousarray(
        b1.reshape(8, 4, 8, 128).transpose(3, 0, 1, 2).reshape(128, 256))
    d["convw"] = np.ascontiguousarray(
        cw.T.reshape(16, 128, 4).transpose(1, 0, 2).reshape(128, 64))
    d["convb"] = np.ascontiguousarray(cb.reshape(16, 128).T)
    d["bdts"] = np.ascontiguousarray(bdt.reshape(16, 128).T)
    d["Dg"] = np.ascontiguousarray(Dv.reshape(16, 128).T)
    d["alog"] = np.ascontiguousarray(
        alog.reshape(16, 128, 16).transpose(1, 0, 2).reshape(128, 256))
    d["ident"] = np.eye(128, dtype=np.float32)
    d["onesc"] = np.ones((128, 1), dtype=np.float32)
    d["onesr"] = np.ones((1, 128), dtype=np.float32)
    sel = np.zeros((16, 2048), dtype=np.float32)
    for s in range(16):
        sel[s, s * 128:(s + 1) * 128] = 1.0
    d["sel16"] = sel
    return d


def kernel(**inputs):
    x = inputs["L0_attn_hidden_states"]
    shared = _prearrange(inputs)
    nc = build_nc()
    in_maps = []
    for core in range(8):
        m = dict(shared)
        m["x"] = np.ascontiguousarray(x[core])
        in_maps.append(m)
    res = run_bass_kernel_spmd(nc, in_maps, core_ids=list(range(8)))
    out = np.stack([np.asarray(res.results[i]["out"]) for i in range(8)], axis=0)
    return out.astype(np.float32, copy=False)


# version 04 (final)
# speedup vs baseline: 1.0114x; 1.0114x over previous
import numpy as np
import concourse.bass as bass
import concourse.mybir as mybir
from concourse.bass_utils import run_bass_kernel_spmd

F32 = mybir.dt.float32
BF16 = mybir.dt.bfloat16
AF = mybir.ActivationFunctionType
ALU = mybir.AluOpType
AX = mybir.AxisListType.X

L = 8192
D = 1024
DI = 2048
NE = 8
DS = 16
NCH = 16
C = 512
TW = 256
NDT = 8
NG = 16
EPS = 1e-5
XRS = 260


def build_nc(nch=NCH):
    nc = bass.Bass("TRN2", target_bir_lowering=False)

    d_x = nc.dram_tensor("x", [L, D], F32, kind="ExternalInput").ap()
    d_w1 = nc.dram_tensor("w1s", [NE, 8, 8, 128, 512], F32, kind="ExternalInput").ap()
    d_w2 = nc.dram_tensor("w2s", [NE, 8, 4, 128, 1024], F32, kind="ExternalInput").ap()
    d_win = nc.dram_tensor("wins", [8, 8, 128, 512], F32, kind="ExternalInput").ap()
    d_wout = nc.dram_tensor("wouts", [4, 16, 128, 256], F32, kind="ExternalInput").ap()
    d_wxdt = nc.dram_tensor("wxdt", [128, 3584], F32, kind="ExternalInput").ap()
    d_gate = nc.dram_tensor("gates", [128, 64], F32, kind="ExternalInput").ap()
    d_b2 = nc.dram_tensor("b2s", [8, 1024], F32, kind="ExternalInput").ap()
    d_b1 = nc.dram_tensor("b1s", [128, 256], F32, kind="ExternalInput").ap()
    d_cw = nc.dram_tensor("convw", [128, 64], F32, kind="ExternalInput").ap()
    d_cb = nc.dram_tensor("convb", [128, 16], F32, kind="ExternalInput").ap()
    d_bdt = nc.dram_tensor("bdts", [128, 16], F32, kind="ExternalInput").ap()
    d_D = nc.dram_tensor("Dg", [128, 16], F32, kind="ExternalInput").ap()
    d_alog = nc.dram_tensor("alog", [128, 256], F32, kind="ExternalInput").ap()
    d_id = nc.dram_tensor("ident", [128, 128], F32, kind="ExternalInput").ap()
    d_oc = nc.dram_tensor("onesc", [128, 1], F32, kind="ExternalInput").ap()
    d_or = nc.dram_tensor("onesr", [1, 128], F32, kind="ExternalInput").ap()
    d_sel = nc.dram_tensor("sel16", [16, 2048], F32, kind="ExternalInput").ap()
    d_out = nc.dram_tensor("out", [L, D], F32, kind="ExternalOutput").ap()

    s_w1 = nc.dram_tensor("w1b", [NE, 8, 8, 128, 512], BF16, kind="Internal").ap()
    s_w2 = nc.dram_tensor("w2b", [NE, 8, 4, 128, 1024], BF16, kind="Internal").ap()
    s_win = nc.dram_tensor("winb", [8, 8, 128, 512], BF16, kind="Internal").ap()
    s_wout = nc.dram_tensor("woutb", [4, 16, 128, 256], BF16, kind="Internal").ap()

    resid = nc.alloc_sbuf_tensor("resid", [128, 8192], F32)
    xnbf = nc.alloc_sbuf_tensor("xnbf", [128, 8192], BF16)
    slots = nc.alloc_sbuf_tensor("slots", [128, 4096], F32)
    slotb = slots[:].bitcast(BF16)
    ysb = nc.alloc_sbuf_tensor("ysb", [128, 4096], F32)
    hq = nc.alloc_sbuf_tensor("hq", [128, 2048], F32)
    hqb = hq[:].bitcast(BF16)
    xr_t = nc.alloc_sbuf_tensor("xr_t", [128, NG * XRS], F32)
    dt_t = nc.alloc_sbuf_tensor("dt_t", [128, 4096], F32)
    ya_t = nc.alloc_sbuf_tensor("ya_t", [128, 4096], F32)
    ybf_t = nc.alloc_sbuf_tensor("ybf_t", [128, 4096], BF16)
    sa_sb = nc.alloc_sbuf_tensor("sa_sb", [128, 512], F32)
    sh_sb = nc.alloc_sbuf_tensor("sh_sb", [128, 512], F32)
    sb1_sb = nc.alloc_sbuf_tensor("sb1_sb", [128, 512], F32)
    st1_sb = nc.alloc_sbuf_tensor("st1_sb", [128, 768], F32)
    cc_sb = nc.alloc_sbuf_tensor("cc_sb", [128, 512], F32)
    bc_sb = nc.alloc_sbuf_tensor("bc_sb", [16, 512], F32)
    dl_sb = nc.alloc_sbuf_tensor("dl_sb", [64, 256], BF16)
    gz_sb = nc.alloc_sbuf_tensor("gz_sb", [128, 512], F32)
    cv_sb = nc.alloc_sbuf_tensor("cv_sb", [128, 512], F32)
    sp_sb = nc.alloc_sbuf_tensor("sp_sb", [128, 1024], F32)
    mt_sb = nc.alloc_sbuf_tensor("mt_sb", [128, 1024], F32)
    ln_sb = nc.alloc_sbuf_tensor("ln_sb", [1, 2048], F32)
    bco_sb = nc.alloc_sbuf_tensor("bco_sb", [1, 512], F32)
    rp_sb = nc.alloc_sbuf_tensor("rp_sb", [128, 32], F32)
    rm_sb = nc.alloc_sbuf_tensor("rm_sb", [128, 12], F32)
    mw_sb = nc.alloc_sbuf_tensor("mw_sb", [128, 32], F32)
    mwt_sb = nc.alloc_sbuf_tensor("mwt_sb", [8, 512], F32)
    wx_sb = nc.alloc_sbuf_tensor("wx_sb", [128, 1536], BF16)
    wdt_sb = nc.alloc_sbuf_tensor("wdt_sb", [64, 2048], BF16)
    gate_sb = nc.alloc_sbuf_tensor("gate_sb", [128, 64], F32)
    b2_sb = nc.alloc_sbuf_tensor("b2_sb", [8, 1024], F32)
    b1_sb = nc.alloc_sbuf_tensor("b1_sb", [128, 256], F32)
    cw_sb = nc.alloc_sbuf_tensor("cw_sb", [128, 64], F32)
    cb_sb = nc.alloc_sbuf_tensor("cb_sb", [128, 16], F32)
    bdt_sb = nc.alloc_sbuf_tensor("bdt_sb", [128, 16], F32)
    D_sb = nc.alloc_sbuf_tensor("D_sb", [128, 16], F32)
    A_sb = nc.alloc_sbuf_tensor("A_sb", [128, 256], F32)
    id_sb = nc.alloc_sbuf_tensor("id_sb", [128, 128], F32)
    oc_sb = nc.alloc_sbuf_tensor("oc_sb", [128, 1], F32)
    or_sb = nc.alloc_sbuf_tensor("or_sb", [1, 128], F32)
    sel_sb = nc.alloc_sbuf_tensor("sel_sb", [16, 2048], F32)
    hst_sb = nc.alloc_sbuf_tensor("hst_sb", [128, 256], F32)
    halo_sb = nc.alloc_sbuf_tensor("halo_sb", [128, 48], F32)

    pb = [nc.alloc_psum_tensor(f"pb{i}", [128, 512], F32) for i in range(8)]

    TE, SE, VE, GP, SY = nc.tensor, nc.scalar, nc.vector, nc.gpsimd, nc.sync

    class Ctr:
        __slots__ = ("h", "n")
        def __init__(self, name):
            self.h = nc.alloc_semaphore(name)
            self.n = 0
        def inc(self, instr, k=1):
            instr.then_inc(self.h, k)
            self.n += k
            return self.n

    cCONST = Ctr("cCONST"); cPR = Ctr("cPR")
    semW = [Ctr(f"semW{i}") for i in range(4)]
    semI = [Ctr(f"semI{i}") for i in range(2)]
    semO = [Ctr(f"semO{i}") for i in range(2)]
    cWF = Ctr("cWF")
    cTE = Ctr("cTE"); cSE = Ctr("cSE"); cVE = Ctr("cVE"); cGP = Ctr("cGP")

    waited = {}
    def wait(eng, ctr, n):
        if n is None or n <= 0:
            return
        key = (id(eng), id(ctr))
        if waited.get(key, -1) >= n:
            return
        waited[key] = n
        eng.wait_ge(ctr.h, n)

    bank_rd = [[] for _ in range(8)]
    def te_take(bi):
        for ctr, n in bank_rd[bi]:
            wait(TE, ctr, n)
        bank_rd[bi] = []
    def read_bank(bi, ctr, n):
        bank_rd[bi].append((ctr, n))

    st = {"piece": 0, "hy": 0, "p67": 0,
          "mt_slot": [None, None], "stage_in": [None, None]}

    def load_piece(src3d, kt, m):
        i = st["piece"]; st["piece"] += 1
        off = (i % 2) * 4096
        wait(SY, cWF, i - 1)
        dst = slotb[:, off:off + kt * m].rearrange("p (a b) -> p a b", a=kt, b=m)
        sw = semW[i % 4]
        SY.dma_start(dst, src3d.rearrange("a p b -> p a b")).then_inc(sw.h, 16)
        sw.n += 16
        return i

    def wait_piece(eng, i):
        wait(eng, semW[i % 4], 16 * (i // 4 + 1))

    def piece_view(i, kt, m):
        off = (i % 2) * 4096
        return slotb[:, off:off + kt * m].rearrange("p (a b) -> p a b", a=kt, b=m)

    with nc.Block():
        for dst, src in [(gate_sb, d_gate), (b2_sb, d_b2), (b1_sb, d_b1),
                         (cw_sb, d_cw), (cb_sb, d_cb), (bdt_sb, d_bdt),
                         (D_sb, d_D), (A_sb, d_alog), (id_sb, d_id),
                         (oc_sb, d_oc), (or_sb, d_or), (sel_sb, d_sel)]:
            SY.dma_start(dst[:], src).then_inc(cCONST.h, 16); cCONST.n += 16
        GP.dma_start(wx_sb[:], d_wxdt[:, 0:1536]).then_inc(cPR.h, 16); cPR.n += 16
        GP.dma_start(wdt_sb[:], d_wxdt[0:64, 1536:3584]).then_inc(cPR.h, 16); cPR.n += 16
        for e in range(NE):
            for q in range(8):
                GP.dma_start(s_w1[e, q], d_w1[e, q]).then_inc(cPR.h, 16); cPR.n += 16
            for q in range(8):
                GP.dma_start(s_w2[e, q], d_w2[e, q]).then_inc(cPR.h, 16); cPR.n += 16
        for q in range(8):
            GP.dma_start(s_win[q], d_win[q]).then_inc(cPR.h, 16); cPR.n += 16
        for h in range(4):
            GP.dma_start(s_wout[h], d_wout[h]).then_inc(cPR.h, 16); cPR.n += 16
        prep_total = cPR.n

        wait(SE, cCONST, cCONST.n); wait(TE, cCONST, cCONST.n)
        wait(VE, cCONST, cCONST.n); wait(GP, cCONST, cCONST.n)
        i = SE.activation(sp_sb[:, 0:256], A_sb[:], AF.Exp)
        m = cSE.inc(i)
        wait(VE, cSE, m)
        VE.tensor_scalar(A_sb[:], sp_sb[:, 0:256], -1.0, None, ALU.mult)
        VE.memset(halo_sb[:], 0.0)
        VE.memset(hst_sb[:], 0.0)
        wait(SY, cPR, prep_total)

        def rs(c):
            return (c % 2) * 4096
        def xs(c):
            return (c % 2) * 4096

        def emit_ln(c, with_router):
            rb = rs(c)
            wait(SE, cTE, cTE.n); wait(SE, cVE, cVE.n); wait(SE, cGP, cGP.n)
            wait(TE, cGP, cGP.n); wait(TE, cVE, cVE.n)
            wait(SE, semO[0], semO[0].n); wait(SE, semO[1], semO[1].n)
            SE.drain(); VE.drain()
            sq_ms = {}
            for dt_ in range(4):
                i = SE.activation(hq[:, (dt_ % 4) * 512:(dt_ % 4) * 512 + 512],
                                  resid[:, rb + dt_ * 512:rb + dt_ * 512 + 512],
                                  AF.Square)
                sq_ms[dt_] = cSE.inc(i)
            te_take(0); te_take(1)
            for dt_ in range(NDT):
                i = TE.matmul(pb[0][0:1, 0:512], oc_sb[:],
                              resid[:, rb + dt_ * 512:rb + dt_ * 512 + 512],
                              start=(dt_ == 0), stop=(dt_ == NDT - 1))
            mm_ms = {}
            for dt_ in range(NDT):
                wait(TE, cSE, sq_ms[dt_])
                i = TE.matmul(pb[1][0:1, 0:512], oc_sb[:],
                              hq[:, (dt_ % 4) * 512:(dt_ % 4) * 512 + 512],
                              start=(dt_ == 0), stop=(dt_ == NDT - 1))
                mm_ms[dt_] = cTE.inc(i)
                nxt = dt_ + 4
                if nxt < NDT:
                    wait(SE, cTE, mm_ms[dt_])
                    if dt_ == 0:
                        SE.drain()
                    i = SE.activation(hq[:, (nxt % 4) * 512:(nxt % 4) * 512 + 512],
                                      resid[:, rb + nxt * 512:rb + nxt * 512 + 512],
                                      AF.Square)
                    sq_ms[nxt] = cSE.inc(i)
            mte = cTE.n
            wait(VE, cTE, mte)
            VE.tensor_scalar(ln_sb[0:1, 1536:2048], pb[1][0:1, 0:512],
                             1.0 / D, None, ALU.mult)
            i = VE.tensor_scalar(ln_sb[0:1, 1024:1536], pb[0][0:1, 0:512],
                                 1.0 / D, None, ALU.mult)
            mve = cVE.inc(i)
            read_bank(0, cVE, mve); read_bank(1, cVE, mve)
            wait(SE, cVE, mve)
            SE.drain()
            i = SE.activation(ln_sb[0:1, 0:512], ln_sb[0:1, 1024:1536], AF.Square)
            mse = cSE.inc(i)
            wait(VE, cSE, mse)
            VE.drain()
            i = VE.scalar_tensor_tensor(ln_sb[0:1, 512:1024], ln_sb[0:1, 1536:2048],
                                        EPS, ln_sb[0:1, 0:512],
                                        ALU.add, ALU.subtract)
            mve = cVE.inc(i)
            wait(SE, cVE, mve)
            SE.activation(ln_sb[0:1, 1536:2048], ln_sb[0:1, 512:1024], AF.Ln)
            SE.drain()
            i = SE.activation(ln_sb[0:1, 0:512], ln_sb[0:1, 1536:2048],
                              AF.Exp, scale=-0.5)
            mse = cSE.inc(i)
            wait(VE, cSE, mse)
            VE.drain()
            i = VE.scalar_tensor_tensor(bco_sb[0:1, 0:512], ln_sb[0:1, 1024:1536],
                                        -1.0, ln_sb[0:1, 0:512], ALU.mult, ALU.mult)
            mve = cVE.inc(i)
            te_take(0); te_take(1)
            wait(TE, cVE, mve)
            TE.matmul(pb[0][:, 0:512], or_sb[:], ln_sb[0:1, 0:512],
                      start=True, stop=True)
            i = TE.matmul(pb[1][:, 0:512], or_sb[:], bco_sb[0:1, 0:512],
                          start=True, stop=True)
            mte = cTE.inc(i)
            wait(VE, cTE, mte)
            VE.drain()
            for dt_ in range(NDT):
                VE.tensor_tensor(ysb[:, dt_ * 512:(dt_ + 1) * 512],
                                 resid[:, rb + dt_ * 512:rb + dt_ * 512 + 512],
                                 pb[0][:, 0:512], ALU.mult)
            VE.drain()
            if with_router:
                for dt_ in range(NDT):
                    i = VE.tensor_tensor(ysb[:, dt_ * 512:(dt_ + 1) * 512],
                                         pb[1][:, 0:512],
                                         ysb[:, dt_ * 512:(dt_ + 1) * 512], ALU.add)
                mve = cVE.inc(i)
                read_bank(0, cVE, mve); read_bank(1, cVE, mve)
                wait(SE, cVE, mve)
                SE.drain()
                for dt_ in range(NDT):
                    i = SE.activation(
                        xnbf[:, xs(c) + dt_ * 512:xs(c) + dt_ * 512 + 512],
                        ysb[:, dt_ * 512:(dt_ + 1) * 512], AF.Identity)
                cSE.inc(i)
                emit_router(c, mve)
            else:
                for dt_ in range(NDT):
                    i = VE.tensor_tensor(
                        xnbf[:, xs(c) + dt_ * 512:xs(c) + dt_ * 512 + 512],
                        ysb[:, dt_ * 512:(dt_ + 1) * 512],
                        pb[1][:, 0:512], ALU.add)
                mve = cVE.inc(i)
                read_bank(0, cVE, mve); read_bank(1, cVE, mve)

        def emit_router(c, m_xn):
            te_take(0)
            wait(TE, cVE, m_xn)
            for tt in range(4):
                for kt in range(NDT):
                    i = TE.matmul(pb[0][:, tt * 8:tt * 8 + 8],
                                  ysb[:, kt * 512 + tt * 128:kt * 512 + tt * 128 + 128],
                                  gate_sb[:, kt * 8:kt * 8 + 8],
                                  start=(kt == 0), stop=(kt == NDT - 1))
            mte = cTE.inc(i)
            wait(VE, cTE, mte)
            VE.drain()
            for tt in range(4):
                VE.reduce_max(rm_sb[:, tt:tt + 1], pb[0][:, tt * 8:tt * 8 + 8], axis=AX)
            VE.drain()
            for tt in range(4):
                i = VE.tensor_scalar(rm_sb[:, 4 + tt:5 + tt], rm_sb[:, tt:tt + 1],
                                     -1.0, None, ALU.mult)
            mve = cVE.inc(i)
            wait(SE, cVE, mve)
            SE.drain()
            for tt in range(4):
                i = SE.activation(rp_sb[:, tt * 8:tt * 8 + 8],
                                  pb[0][:, tt * 8:tt * 8 + 8],
                                  AF.Exp, bias=rm_sb[:, 4 + tt:5 + tt])
            mse = cSE.inc(i)
            wait(VE, cSE, mse)
            VE.drain()
            for tt in range(4):
                VE.reduce_sum(rm_sb[:, 8 + tt:9 + tt], rp_sb[:, tt * 8:tt * 8 + 8],
                              axis=AX)
            VE.drain()
            for tt in range(4):
                VE.reciprocal(rm_sb[:, 8 + tt:9 + tt], rm_sb[:, 8 + tt:9 + tt])
            VE.drain()
            for tt in range(4):
                VE.tensor_scalar(rm_sb[:, 8 + tt:9 + tt], rm_sb[:, 8 + tt:9 + tt],
                                 0.5, None, ALU.mult)
            VE.drain()
            for tt in range(4):
                i = VE.tensor_scalar(mw_sb[:, tt * 8:tt * 8 + 8],
                                     pb[0][:, tt * 8:tt * 8 + 8],
                                     rm_sb[:, tt:tt + 1], rm_sb[:, 8 + tt:9 + tt],
                                     ALU.is_equal, ALU.mult)
            mve = cVE.inc(i)
            read_bank(0, cVE, mve)
            te_take(1)
            wait(TE, cVE, mve)
            for tt in range(4):
                i = TE.transpose(pb[1][0:8, tt * 128:(tt + 1) * 128],
                                 mw_sb[:, tt * 8:tt * 8 + 8], id_sb[:])
            mte = cTE.inc(i)
            wait(SE, cTE, mte)
            SE.drain()
            i = SE.activation(mwt_sb[0:8, 0:512], pb[1][0:8, 0:512], AF.Identity)
            mse = cSE.inc(i)
            read_bank(1, cSE, mse)

        def emit_ffn(c, pump):
            rb = rs(c)
            xb = xs(c)
            m_mwt = cSE.n
            for j in range(NDT):
                bi = j % 2
                te_take(bi)
                wait(TE, cSE, m_mwt)
                i = TE.matmul(pb[bi][:, 0:512], b2_sb[0:8, j * 128:(j + 1) * 128],
                              mwt_sb[0:8, 0:512], start=True, stop=True)
                mte = cTE.inc(i)
                wait(VE, cTE, mte)
                if j < 2:
                    VE.drain()
                r = resid[:, rb + j * 512:rb + j * 512 + 512]
                i = VE.tensor_tensor(r, pb[bi][:, 0:512], r, ALU.add)
                mve = cVE.inc(i)
                read_bank(bi, cVE, mve)
            for e in range(NE):
                bi = 4 + e % 2
                te_take(bi)
                i = TE.matmul(pb[bi][:, 0:512], sel_sb[0:8, e * 128:(e + 1) * 128],
                              mwt_sb[0:8, 0:512], start=True, stop=True)
                m_mbc = cTE.inc(i)
                for p8 in range(8):
                    p1 = load_piece(s_w1[e, p8], 8, 512)
                    w1v = piece_view(p1, 8, 512)
                    wait_piece(TE, p1)
                    hslot = (p8 % 2) * 2048
                    for j in range(4):
                        hb = st["hy"] % 2; st["hy"] += 1
                        te_take(hb)
                        for kt in range(8):
                            i = TE.matmul(pb[hb][:, 0:512],
                                          w1v[:, kt, j * 128:(j + 1) * 128],
                                          xnbf[:, xb + kt * 512:xb + kt * 512 + 512],
                                          start=(kt == 0), stop=(kt == 7))
                        mte = cTE.inc(i)
                        if j == 3:
                            cWF.inc(TE.drain())
                        wait(SE, cTE, mte)
                        if j == 0:
                            SE.drain()
                        bcol = e * 32 + p8 * 4 + j
                        i = SE.activation(hqb[:, hslot + j * 512:hslot + j * 512 + 512],
                                          pb[hb][:, 0:512], AF.Gelu,
                                          bias=b1_sb[:, bcol:bcol + 1])
                        mse = cSE.inc(i)
                        read_bank(hb, cSE, mse)
                    m_h = cSE.n
                    pump(3)
                    p2 = load_piece(s_w2[e, p8], 4, 1024)
                    w2v = piece_view(p2, 4, 1024)
                    wait_piece(TE, p2)
                    wait(TE, cSE, m_h)
                    for j2 in range(NDT):
                        yb = st["hy"] % 2; st["hy"] += 1
                        te_take(yb)
                        for kt in range(4):
                            i = TE.matmul(pb[yb][:, 0:512],
                                          w2v[:, kt, j2 * 128:(j2 + 1) * 128],
                                          hqb[:, hslot + kt * 512:hslot + kt * 512 + 512],
                                          start=(kt == 0), stop=(kt == 3))
                        mte = cTE.inc(i)
                        if j2 == NDT - 1:
                            cWF.inc(TE.drain())
                        wait(VE, cTE, mte)
                        if j2 == 0:
                            VE.drain()
                        ys = ysb[:, j2 * 512:(j2 + 1) * 512]
                        if p8 == 0:
                            i = VE.tensor_copy(ys, pb[yb][:, 0:512])
                        else:
                            i = VE.tensor_tensor(ys, pb[yb][:, 0:512], ys, ALU.add)
                        mve = cVE.inc(i)
                        read_bank(yb, cVE, mve)
                    pump(3)
                wait(VE, cTE, m_mbc)
                VE.drain()
                for j in range(NDT):
                    sl = (j % 2) * 512
                    wait(VE, cGP, st["mt_slot"][j % 2])
                    VE.drain()
                    i = VE.tensor_tensor(mt_sb[:, sl:sl + 512],
                                         ysb[:, j * 512:(j + 1) * 512],
                                         pb[bi][:, 0:512], ALU.mult)
                    mve = cVE.inc(i)
                    wait(GP, cVE, mve)
                    r = resid[:, rb + j * 512:rb + j * 512 + 512]
                    i = GP.tensor_tensor(r, mt_sb[:, sl:sl + 512], r, ALU.add)
                    st["mt_slot"][j % 2] = cGP.inc(i)
                read_bank(bi, cVE, cVE.n)

        def xrw(g, a, b):
            return xr_t[:, g * XRS + a:g * XRS + b]
        def yaw(g):
            return ya_t[:, g * 256:(g + 1) * 256]
        def st1s(t):
            return st1_sb[:, (t % 3) * 256:(t % 3) * 256 + 256]

        def p67_take():
            bi = 6 + st["p67"] % 2; st["p67"] += 1
            te_take(bi)
            return bi

        def mamba_gen(c, m_xn_ve):
            rb = rs(c)
            xb = xs(c)
            for w in range(2):
                t0 = w * 256
                xz_ms = {}
                m_bprev = st.get("m_scanb")
                for pc in range(4):
                    p = load_piece(s_win[pc], 8, 512)
                    wv = piece_view(p, 8, 512)
                    wait_piece(TE, p)
                    wait(TE, cVE, m_xn_ve)
                    for mt in range(4):
                        bi = p67_take()
                        for kt in range(8):
                            i = TE.matmul(
                                pb[bi][:, 0:256], wv[:, kt, mt * 128:(mt + 1) * 128],
                                xnbf[:, xb + kt * 512 + t0:xb + kt * 512 + t0 + 256],
                                start=(kt == 0), stop=(kt == 7))
                        mte = cTE.inc(i)
                        if mt == 3:
                            cWF.inc(TE.drain())
                        wait(SE, cTE, mte)
                        if pc == 0 and mt == 0:
                            wait(SE, cVE, m_bprev)
                            SE.drain()
                        gi = pc * 4 + mt
                        i = SE.activation(xrw(gi, 3, 259), pb[bi][:, 0:256],
                                          AF.Identity)
                        xz_ms[gi] = cSE.inc(i)
                        read_bank(bi, cSE, xz_ms[gi])
                    yield
                wait(VE, cSE, xz_ms[NG - 1])
                VE.drain()
                xr_g = xr_t[:, 0:NG * XRS].rearrange("p (g a) -> p g a", g=NG, a=XRS)
                halo_g = halo_sb[:].rearrange("p (g a) -> p g a", g=NG, a=3)
                VE.tensor_copy(xr_g[:, :, 0:3], halo_g)
                VE.drain()
                VE.tensor_copy(halo_g, xr_g[:, :, 256:259])
                VE.drain()
                silu_ms = {}
                for g0 in range(0, NG, 2):
                    if g0 >= 2:
                        wait(VE, cSE, silu_ms[g0 - 1])
                        VE.drain()
                    for g in (g0, g0 + 1):
                        sl = (g % 2) * 256
                        VE.tensor_scalar(cv_sb[:, sl:sl + 256], xrw(g, 0, 256),
                                         cw_sb[:, g * 4:g * 4 + 1], None, ALU.mult)
                    for k in range(1, 4):
                        VE.drain()
                        for g in (g0, g0 + 1):
                            sl = (g % 2) * 256
                            i = VE.scalar_tensor_tensor(
                                cv_sb[:, sl:sl + 256], xrw(g, k, 256 + k),
                                cw_sb[:, g * 4 + k:g * 4 + k + 1],
                                cv_sb[:, sl:sl + 256], ALU.mult, ALU.add)
                    mve = cVE.inc(i)
                    wait(SE, cVE, mve)
                    if g0 == 0:
                        SE.drain()
                    for g in (g0, g0 + 1):
                        sl = (g % 2) * 256
                        i = SE.activation(xrw(g, 3, 259), cv_sb[:, sl:sl + 256],
                                          AF.Silu, bias=cb_sb[:, g:g + 1])
                        silu_ms[g] = cSE.inc(i)
                    if g0 % 4 == 2:
                        yield
                SE.drain()
                for g in range(NG):
                    i = SE.activation(ybf_t[:, g * 256:(g + 1) * 256],
                                      xrw(g, 3, 259), AF.Identity)
                m_xbf = cSE.inc(i)
                yield
                te_take(6); te_take(7)
                wait(TE, cSE, m_xbf)
                for kt in range(NG):
                    TE.matmul(pb[6][0:64, 0:256], wx_sb[:, kt * 96:kt * 96 + 64],
                              ybf_t[:, kt * 256:(kt + 1) * 256],
                              start=(kt == 0), stop=(kt == NG - 1))
                for kt in range(NG):
                    TE.matmul(pb[7][0:16, 0:256], wx_sb[:, kt * 96 + 64:kt * 96 + 80],
                              ybf_t[:, kt * 256:(kt + 1) * 256],
                              start=(kt == 0), stop=(kt == NG - 1))
                for kt in range(NG):
                    i = TE.matmul(pb[7][0:16, 256:512],
                                  wx_sb[:, kt * 96 + 80:kt * 96 + 96],
                                  ybf_t[:, kt * 256:(kt + 1) * 256],
                                  start=(kt == 0), stop=(kt == NG - 1))
                mte = cTE.inc(i)
                wait(SE, cTE, mte)
                SE.drain()
                SE.activation(dl_sb[:], pb[6][0:64, 0:256], AF.Identity)
                i = SE.activation(bc_sb[0:16, 0:512], pb[7][0:16, 0:512], AF.Identity)
                m_bcs = cSE.inc(i)
                read_bank(6, cSE, m_bcs); read_bank(7, cSE, m_bcs)
                yield
                dtv_ms = {}
                for mt0 in range(0, NG, 2):
                    bis = []
                    for mt in (mt0, mt0 + 1):
                        bi = p67_take()
                        wait(TE, cSE, m_bcs)
                        i = TE.matmul(pb[bi][0:128, 0:256],
                                      wdt_sb[:, mt * 128:(mt + 1) * 128],
                                      dl_sb[:], start=True, stop=True)
                        bis.append(bi)
                    mte = cTE.inc(i)
                    wait(SE, cTE, mte)
                    if mt0 >= 2:
                        wait(SE, cVE, dtv_ms[mt0 - 1])
                    SE.drain()
                    for k, mt in enumerate((mt0, mt0 + 1)):
                        SE.activation(sp_sb[:, k * 256:k * 256 + 256],
                                      pb[bis[k]][:, 0:256], AF.Abs,
                                      bias=bdt_sb[:, mt:mt + 1])
                    SE.drain()
                    for k in range(2):
                        SE.activation(sp_sb[:, 512 + k * 256:512 + k * 256 + 256],
                                      sp_sb[:, k * 256:k * 256 + 256],
                                      AF.Exp, scale=-1.0)
                    SE.drain()
                    for k in range(2):
                        SE.activation(sp_sb[:, k * 256:k * 256 + 256],
                                      sp_sb[:, 512 + k * 256:512 + k * 256 + 256],
                                      AF.Ln, bias=1.0)
                    SE.drain()
                    for k, mt in enumerate((mt0, mt0 + 1)):
                        i = SE.activation(dt_t[:, mt * 256:(mt + 1) * 256],
                                          pb[bis[k]][:, 0:256],
                                          AF.Relu, bias=bdt_sb[:, mt:mt + 1])
                    mse = cSE.inc(i)
                    read_bank(bis[0], cSE, mse); read_bank(bis[1], cSE, mse)
                    wait(VE, cSE, mse)
                    VE.drain()
                    for k, mt in enumerate((mt0, mt0 + 1)):
                        i = VE.tensor_tensor(dt_t[:, mt * 256:(mt + 1) * 256],
                                             sp_sb[:, k * 256:k * 256 + 256],
                                             dt_t[:, mt * 256:(mt + 1) * 256],
                                             ALU.add)
                        dtv_ms[mt] = cVE.inc(i)
                    if mt0 % 4 == 2:
                        yield
                VE.drain()
                for g0 in range(0, NG, 2):
                    for g in (g0, g0 + 1):
                        VE.tensor_scalar(yaw(g), xrw(g, 3, 259), D_sb[:, g:g + 1],
                                         None, ALU.mult)
                    VE.drain()
                    for g in (g0, g0 + 1):
                        i = VE.tensor_tensor(xrw(g, 3, 259),
                                             dt_t[:, g * 256:(g + 1) * 256],
                                             xrw(g, 3, 259), ALU.mult)
                m_u = cVE.inc(i)
                VE.drain()
                yield
                base_gp = cGP.n
                scan_ms = {}
                gp_ms = {}
                pend = []
                if True:
                    GP.drain()
                for s in range(DS):
                    bB = 6 + (s % 2)
                    te_take(bB)
                    TE.matmul(pb[bB][:, 0:256], sel_sb[0:16, s * 128:(s + 1) * 128],
                              bc_sb[0:16, 0:256], start=True, stop=True)
                    i = TE.matmul(pb[bB][:, 256:512],
                                  sel_sb[0:16, s * 128:(s + 1) * 128],
                                  bc_sb[0:16, 256:512], start=True, stop=True)
                    mte = cTE.inc(i)
                    csl = (s % 2) * 256
                    if s >= 2:
                        wait(SE, cGP, gp_ms[(s - 2) * NG + (NG - 1)])
                    wait(SE, cTE, mte)
                    SE.drain()
                    i = SE.activation(cc_sb[:, csl:csl + 256], pb[bB][:, 256:512],
                                      AF.Identity)
                    msec = cSE.inc(i)
                    read_bank(bB, cSE, msec)
                    for g in range(NG):
                        t = s * NG + g
                        asl = (g % 2) * 256
                        wait(SE, cVE, dtv_ms[g])
                        if t >= 2:
                            wait(SE, cVE, scan_ms[t - 2])
                        SE.drain()
                        i = SE.activation(sa_sb[:, asl:asl + 256],
                                          dt_t[:, g * 256:(g + 1) * 256], AF.Exp,
                                          scale=A_sb[:, g * 16 + s:g * 16 + s + 1])
                        msea = cSE.inc(i)
                        wait(VE, cSE, msea)
                        wait(VE, cTE, mte)
                        ib = VE.tensor_tensor(sb1_sb[:, (t % 2) * 256:(t % 2) * 256 + 256],
                                              xrw(g, 3, 259), pb[bB][:, 0:256],
                                              ALU.mult)
                        mvb = cVE.inc(ib)
                        for lag in list(pend):
                            if lag[0] <= t - 2:
                                pend.remove(lag)
                                wait(VE, cGP, gp_ms[lag[0]])
                                i2 = VE.tensor_tensor(yaw(lag[1]), st1s(lag[0]),
                                                      yaw(lag[1]), ALU.add)
                                cVE.inc(i2)
                        wait(VE, cVE, mvb)
                        hsl = (g % 2) * 256
                        if t >= 2:
                            wait(VE, cGP, gp_ms[t - 2])
                        i = VE.tensor_tensor_scan(
                            sh_sb[:, hsl:hsl + 256], sa_sb[:, asl:asl + 256],
                            sb1_sb[:, (t % 2) * 256:(t % 2) * 256 + 256],
                            hst_sb[:, g * 16 + s:g * 16 + s + 1],
                            ALU.mult, ALU.add)
                        scan_ms[t] = cVE.inc(i)
                        wait(GP, cVE, scan_ms[t])
                        wait(GP, cSE, msec)
                        GP.tensor_tensor(st1s(t), sh_sb[:, hsl:hsl + 256],
                                         cc_sb[:, csl:csl + 256], ALU.mult)
                        i = GP.tensor_copy(hst_sb[:, g * 16 + s:g * 16 + s + 1],
                                           sh_sb[:, hsl + 255:hsl + 256])
                        gp_ms[t] = cGP.inc(i)
                        pend.append((t, g))
                        if g % 4 == 3:
                            yield
                    read_bank(bB, cVE, cVE.n)
                st["m_scanb"] = cVE.n
                for lag in pend:
                    wait(VE, cGP, gp_ms[lag[0]])
                    i = VE.tensor_tensor(yaw(lag[1]), st1s(lag[0]),
                                         yaw(lag[1]), ALU.add)
                    cVE.inc(i)
                m_scan = cVE.n
                yield
                gt_ms = {}
                for pc in range(4):
                    p = load_piece(s_win[4 + pc], 8, 512)
                    wv = piece_view(p, 8, 512)
                    wait_piece(TE, p)
                    for mt in range(4):
                        bi = p67_take()
                        for kt in range(8):
                            i = TE.matmul(
                                pb[bi][:, 0:256], wv[:, kt, mt * 128:(mt + 1) * 128],
                                xnbf[:, xb + kt * 512 + t0:xb + kt * 512 + t0 + 256],
                                start=(kt == 0), stop=(kt == 7))
                        mte = cTE.inc(i)
                        if mt == 3:
                            cWF.inc(TE.drain())
                        gi = pc * 4 + mt
                        wait(SE, cTE, mte)
                        if gi >= 2:
                            wait(SE, cVE, gt_ms[gi - 2])
                        SE.drain()
                        sl = (gi % 2) * 256
                        i = SE.activation(gz_sb[:, sl:sl + 256], pb[bi][:, 0:256],
                                          AF.Silu)
                        mse = cSE.inc(i)
                        read_bank(bi, cSE, mse)
                        wait(VE, cSE, mse)
                        if gi == 0:
                            wait(VE, cVE, m_scan)
                            VE.drain()
                        i = VE.tensor_tensor(ybf_t[:, gi * 256:(gi + 1) * 256],
                                             yaw(gi), gz_sb[:, sl:sl + 256],
                                             ALU.mult)
                        gt_ms[gi] = cVE.inc(i)
                    yield
                m_ybf = cVE.n
                for pc in range(4):
                    p = load_piece(s_wout[pc], 16, 256)
                    wv = piece_view(p, 16, 256)
                    wait_piece(TE, p)
                    wait(TE, cVE, m_ybf)
                    for mt in range(2):
                        bi = p67_take()
                        for kt in range(NG):
                            i = TE.matmul(pb[bi][:, 0:256],
                                          wv[:, kt, mt * 128:(mt + 1) * 128],
                                          ybf_t[:, kt * 256:(kt + 1) * 256],
                                          start=(kt == 0), stop=(kt == NG - 1))
                        mte = cTE.inc(i)
                        if mt == 1:
                            cWF.inc(TE.drain())
                        wait(VE, cTE, mte)
                        dt_i = pc * 2 + mt
                        if dt_i == 0:
                            VE.drain()
                        r = resid[:, rb + dt_i * 512 + t0:rb + dt_i * 512 + t0 + 256]
                        i = VE.tensor_tensor(r, pb[bi][:, 0:256], r, ALU.add)
                        mve = cVE.inc(i)
                        read_bank(bi, cVE, mve)
                    if pc % 2 == 1:
                        yield

        def emit_in(c):
            wait(SY, semO[0], semO[0].n)
            wait(SY, semO[1], semO[1].n)
            wait(SY, cSE, cSE.n)
            wait(SY, cTE, cTE.n)
            rb = rs(c)
            for tt in range(4):
                g = c * 4 + tt
                wait(SY, cSE, st["stage_in"][tt % 2])
                si = semI[g % 2]
                SY.dma_start(hq[:, (tt % 2) * 1024:(tt % 2) * 1024 + 1024],
                             d_x[g * 128:(g + 1) * 128, :]).then_inc(si.h, 16)
                si.n += 16
                wait(TE, si, si.n)
                for dt_ in range(NDT):
                    bi = dt_ % 2
                    te_take(bi)
                    i = TE.transpose(pb[bi][:, 0:128],
                                     hq[:, (tt % 2) * 1024 + dt_ * 128:
                                         (tt % 2) * 1024 + dt_ * 128 + 128],
                                     id_sb[:])
                    mte = cTE.inc(i)
                    wait(SE, cTE, mte)
                    if dt_ == 0:
                        SE.drain()
                    i = SE.activation(
                        resid[:, rb + dt_ * 512 + tt * 128:rb + dt_ * 512 + tt * 128 + 128],
                        pb[bi][:, 0:128], AF.Identity)
                    mse = cSE.inc(i)
                    read_bank(bi, cSE, mse)
                st["stage_in"][tt % 2] = mse

        def emit_out(c):
            rb = rs(c)
            wait(TE, cGP, cGP.n)
            wait(SE, cGP, cGP.n)
            for tt in range(4):
                g = c * 4 + tt
                so = semO[g % 2]
                for dt_ in range(NDT):
                    bi = dt_ % 2
                    te_take(bi)
                    i = TE.transpose(
                        pb[bi][:, 0:128],
                        resid[:, rb + dt_ * 512 + tt * 128:rb + dt_ * 512 + tt * 128 + 128],
                        id_sb[:])
                    mte = cTE.inc(i)
                    wait(SE, cTE, mte)
                    if dt_ == 0:
                        wait(SE, so, so.n)
                        SE.drain()
                    i = SE.activation(hq[:, (tt % 2) * 1024 + dt_ * 128:
                                          (tt % 2) * 1024 + dt_ * 128 + 128],
                                      pb[bi][:, 0:128], AF.Identity)
                    mse = cSE.inc(i)
                    read_bank(bi, cSE, mse)
                wait(SY, cSE, mse)
                SY.dma_start(d_out[g * 128:(g + 1) * 128, :],
                             hq[:, (tt % 2) * 1024:(tt % 2) * 1024 + 1024]
                             ).then_inc(so.h, 16)
                so.n += 16

        st["gen"] = None
        def pump(k):
            g = st["gen"]
            if g is None:
                return
            for _ in range(k):
                try:
                    next(g)
                except StopIteration:
                    st["gen"] = None
                    return
        def flush():
            g = st["gen"]
            if g is not None:
                for _ in g:
                    pass
                st["gen"] = None

        for c in range(nch):
            emit_in(c)
            emit_ln(c, True)
            emit_ffn(c, pump)
            emit_ln(c, False)
            flush()
            st["gen"] = mamba_gen(c, cVE.n)
            if c >= 1:
                j = c - 1
                emit_ln(j, True)
                emit_ffn(j, pump)
                emit_out(j)
        flush()
        emit_ln(nch - 1, True)
        emit_ffn(nch - 1, lambda k: None)
        emit_out(nch - 1)
        SY.wait_ge(semO[0].h, semO[0].n)
        SY.wait_ge(semO[1].h, semO[1].n)

    return nc


def _prearrange(inputs):
    w1 = inputs["L0_ffn_w1"]
    w2 = inputs["L0_ffn_w2"]
    win = inputs["L0_attn_W_in"]
    wout = inputs["L0_attn_W_out"]
    wx = inputs["L0_attn_W_x"]
    wdt = inputs["L0_attn_W_dt"]
    gate = inputs["L0_ffn_gate_w"]
    b1 = inputs["L0_ffn_b1"]
    b2 = inputs["L0_ffn_b2"]
    cw = inputs["L0_attn_conv_w"]
    cb = inputs["L0_attn_conv_b"]
    bdt = inputs["L0_attn_b_dt"]
    Dv = inputs["L0_attn_D"]
    alog = inputs["L0_attn_A_log"]

    d = {}
    d["w1s"] = np.ascontiguousarray(
        w1.reshape(8, 8, 128, 8, 512).transpose(0, 3, 1, 2, 4))
    d["w2s"] = np.ascontiguousarray(w2.reshape(8, 8, 4, 128, 1024))
    d["wins"] = np.ascontiguousarray(
        win.reshape(8, 128, 8, 512).transpose(2, 0, 1, 3))
    d["wouts"] = np.ascontiguousarray(
        wout.reshape(16, 128, 4, 256).transpose(2, 0, 1, 3))
    wxdt = np.zeros((128, 3584), dtype=np.float32)
    wxdt[:, 0:1536] = wx.reshape(16, 128, 96).transpose(1, 0, 2).reshape(128, 1536)
    wxdt[0:64, 1536:3584] = wdt
    d["wxdt"] = wxdt
    d["gates"] = np.ascontiguousarray(
        gate.reshape(8, 128, 8).transpose(1, 0, 2).reshape(128, 64))
    d["b2s"] = np.ascontiguousarray(b2)
    d["b1s"] = np.ascontiguousarray(
        b1.reshape(8, 8, 4, 128).transpose(3, 0, 1, 2).reshape(128, 256))
    d["convw"] = np.ascontiguousarray(
        cw.T.reshape(16, 128, 4).transpose(1, 0, 2).reshape(128, 64))
    d["convb"] = np.ascontiguousarray(cb.reshape(16, 128).T)
    d["bdts"] = np.ascontiguousarray(bdt.reshape(16, 128).T)
    d["Dg"] = np.ascontiguousarray(Dv.reshape(16, 128).T)
    d["alog"] = np.ascontiguousarray(
        alog.reshape(16, 128, 16).transpose(1, 0, 2).reshape(128, 256))
    d["ident"] = np.eye(128, dtype=np.float32)
    d["onesc"] = np.ones((128, 1), dtype=np.float32)
    d["onesr"] = np.ones((1, 128), dtype=np.float32)
    sel = np.zeros((16, 2048), dtype=np.float32)
    for s in range(16):
        sel[s, s * 128:(s + 1) * 128] = 1.0
    d["sel16"] = sel
    return d


def kernel(**inputs):
    x = inputs["L0_attn_hidden_states"]
    shared = _prearrange(inputs)
    nc = build_nc()
    in_maps = []
    for core in range(8):
        m = dict(shared)
        m["x"] = np.ascontiguousarray(x[core])
        in_maps.append(m)
    res = run_bass_kernel_spmd(nc, in_maps, core_ids=list(range(8)))
    out = np.stack([np.asarray(res.results[i]["out"]) for i in range(8)], axis=0)
    return out.astype(np.float32, copy=False)


# version 05
# speedup vs baseline: 1.0135x; 1.0021x over previous
import numpy as np
import concourse.bass as bass
import concourse.mybir as mybir
from concourse.bass_utils import run_bass_kernel_spmd

F32 = mybir.dt.float32
BF16 = mybir.dt.bfloat16
AF = mybir.ActivationFunctionType
ALU = mybir.AluOpType
AX = mybir.AxisListType.X

L = 8192
D = 1024
DI = 2048
NE = 8
DS = 16
NCH = 16
C = 512
TW = 256
NDT = 8
NG = 16
EPS = 1e-5
XRS = 260


def build_nc(nch=NCH):
    nc = bass.Bass("TRN2", target_bir_lowering=False)

    d_x = nc.dram_tensor("x", [L, D], F32, kind="ExternalInput").ap()
    d_w1 = nc.dram_tensor("w1s", [NE, 8, 8, 128, 512], F32, kind="ExternalInput").ap()
    d_w2 = nc.dram_tensor("w2s", [NE, 8, 4, 128, 1024], F32, kind="ExternalInput").ap()
    d_win = nc.dram_tensor("wins", [8, 8, 128, 512], F32, kind="ExternalInput").ap()
    d_wout = nc.dram_tensor("wouts", [4, 16, 128, 256], F32, kind="ExternalInput").ap()
    d_wxdt = nc.dram_tensor("wxdt", [128, 3584], F32, kind="ExternalInput").ap()
    d_gate = nc.dram_tensor("gates", [128, 64], F32, kind="ExternalInput").ap()
    d_b2 = nc.dram_tensor("b2s", [8, 1024], F32, kind="ExternalInput").ap()
    d_b1 = nc.dram_tensor("b1s", [128, 256], F32, kind="ExternalInput").ap()
    d_cw = nc.dram_tensor("convw", [128, 64], F32, kind="ExternalInput").ap()
    d_cb = nc.dram_tensor("convb", [128, 16], F32, kind="ExternalInput").ap()
    d_bdt = nc.dram_tensor("bdts", [128, 16], F32, kind="ExternalInput").ap()
    d_D = nc.dram_tensor("Dg", [128, 16], F32, kind="ExternalInput").ap()
    d_alog = nc.dram_tensor("alog", [128, 256], F32, kind="ExternalInput").ap()
    d_id = nc.dram_tensor("ident", [128, 128], F32, kind="ExternalInput").ap()
    d_oc = nc.dram_tensor("onesc", [128, 1], F32, kind="ExternalInput").ap()
    d_or = nc.dram_tensor("onesr", [1, 128], F32, kind="ExternalInput").ap()
    d_sel = nc.dram_tensor("sel16", [16, 2048], F32, kind="ExternalInput").ap()
    d_out = nc.dram_tensor("out", [L, D], F32, kind="ExternalOutput").ap()

    s_w1 = nc.dram_tensor("w1b", [NE, 8, 8, 128, 512], BF16, kind="Internal").ap()
    s_w2 = nc.dram_tensor("w2b", [NE, 8, 4, 128, 1024], BF16, kind="Internal").ap()
    s_win = nc.dram_tensor("winb", [8, 8, 128, 512], BF16, kind="Internal").ap()
    s_wout = nc.dram_tensor("woutb", [4, 16, 128, 256], BF16, kind="Internal").ap()

    resid = nc.alloc_sbuf_tensor("resid", [128, 8192], F32)
    xnbf = nc.alloc_sbuf_tensor("xnbf", [128, 8192], BF16)
    slots = nc.alloc_sbuf_tensor("slots", [128, 4096], F32)
    slotb = slots[:].bitcast(BF16)
    ysb = nc.alloc_sbuf_tensor("ysb", [128, 4096], F32)
    hq = nc.alloc_sbuf_tensor("hq", [128, 2048], F32)
    hqb = hq[:].bitcast(BF16)
    xr_t = nc.alloc_sbuf_tensor("xr_t", [128, NG * XRS], F32)
    dt_t = nc.alloc_sbuf_tensor("dt_t", [128, 4096], F32)
    ya_t = nc.alloc_sbuf_tensor("ya_t", [128, 4096], F32)
    ybf_t = nc.alloc_sbuf_tensor("ybf_t", [128, 4096], BF16)
    sa_sb = nc.alloc_sbuf_tensor("sa_sb", [128, 768], F32)
    sh_sb = nc.alloc_sbuf_tensor("sh_sb", [128, 512], F32)
    sb1_sb = nc.alloc_sbuf_tensor("sb1_sb", [128, 768], F32)
    st1_sb = nc.alloc_sbuf_tensor("st1_sb", [128, 768], F32)
    cc_sb = nc.alloc_sbuf_tensor("cc_sb", [128, 1024], F32)
    bc_sb = nc.alloc_sbuf_tensor("bc_sb", [16, 512], F32)
    dl_sb = nc.alloc_sbuf_tensor("dl_sb", [64, 256], BF16)
    gz_sb = nc.alloc_sbuf_tensor("gz_sb", [128, 512], F32)
    cv_sb = nc.alloc_sbuf_tensor("cv_sb", [128, 512], F32)
    sp_sb = nc.alloc_sbuf_tensor("sp_sb", [128, 1024], F32)
    mt_sb = nc.alloc_sbuf_tensor("mt_sb", [128, 1024], F32)
    ln_sb = nc.alloc_sbuf_tensor("ln_sb", [1, 1536], F32)
    bco_sb = nc.alloc_sbuf_tensor("bco_sb", [1, 512], F32)
    rp_sb = nc.alloc_sbuf_tensor("rp_sb", [128, 32], F32)
    rm_sb = nc.alloc_sbuf_tensor("rm_sb", [128, 12], F32)
    mw_sb = mt_sb
    mwt_sb = nc.alloc_sbuf_tensor("mwt_sb", [8, 512], F32)
    wx_sb = nc.alloc_sbuf_tensor("wx_sb", [128, 1536], BF16)
    wdt_sb = nc.alloc_sbuf_tensor("wdt_sb", [64, 2048], BF16)
    gate_sb = nc.alloc_sbuf_tensor("gate_sb", [128, 64], F32)
    b2_sb = nc.alloc_sbuf_tensor("b2_sb", [8, 1024], F32)
    b1_sb = nc.alloc_sbuf_tensor("b1_sb", [128, 256], F32)
    cw_sb = nc.alloc_sbuf_tensor("cw_sb", [128, 64], F32)
    cb_sb = nc.alloc_sbuf_tensor("cb_sb", [128, 16], F32)
    bdt_sb = nc.alloc_sbuf_tensor("bdt_sb", [128, 16], F32)
    D_sb = nc.alloc_sbuf_tensor("D_sb", [128, 16], F32)
    A_sb = nc.alloc_sbuf_tensor("A_sb", [128, 256], F32)
    id_sb = nc.alloc_sbuf_tensor("id_sb", [128, 128], F32)
    oc_sb = nc.alloc_sbuf_tensor("oc_sb", [128, 1], F32)
    or_sb = nc.alloc_sbuf_tensor("or_sb", [1, 128], F32)
    sel_sb = nc.alloc_sbuf_tensor("sel_sb", [16, 2048], F32)
    hst_sb = nc.alloc_sbuf_tensor("hst_sb", [128, 256], F32)
    halo_sb = nc.alloc_sbuf_tensor("halo_sb", [128, 48], F32)

    pb = [nc.alloc_psum_tensor(f"pb{i}", [128, 512], F32) for i in range(8)]

    TE, SE, VE, GP, SY = nc.tensor, nc.scalar, nc.vector, nc.gpsimd, nc.sync

    class Ctr:
        __slots__ = ("h", "n")
        def __init__(self, name):
            self.h = nc.alloc_semaphore(name)
            self.n = 0
        def inc(self, instr, k=1):
            instr.then_inc(self.h, k)
            self.n += k
            return self.n

    cCONST = Ctr("cCONST"); cPR = Ctr("cPR")
    semW = [Ctr(f"semW{i}") for i in range(4)]
    semI = [Ctr(f"semI{i}") for i in range(2)]
    semO = [Ctr(f"semO{i}") for i in range(2)]
    cWF = Ctr("cWF")
    cTE = Ctr("cTE"); cSE = Ctr("cSE"); cVE = Ctr("cVE"); cGP = Ctr("cGP")

    waited = {}
    def wait(eng, ctr, n):
        if n is None or n <= 0:
            return
        key = (id(eng), id(ctr))
        if waited.get(key, -1) >= n:
            return
        waited[key] = n
        eng.wait_ge(ctr.h, n)

    bank_rd = [[] for _ in range(8)]
    def te_take(bi):
        for ctr, n in bank_rd[bi]:
            wait(TE, ctr, n)
        bank_rd[bi] = []
    def read_bank(bi, ctr, n):
        bank_rd[bi].append((ctr, n))

    st = {"piece": 0, "hy": 0, "p67": 0,
          "mt_slot": [None, None], "stage_in": [None, None]}

    def load_piece(src3d, kt, m):
        i = st["piece"]; st["piece"] += 1
        off = (i % 2) * 4096
        wait(SY, cWF, i - 1)
        dst = slotb[:, off:off + kt * m].rearrange("p (a b) -> p a b", a=kt, b=m)
        sw = semW[i % 4]
        SY.dma_start(dst, src3d.rearrange("a p b -> p a b")).then_inc(sw.h, 16)
        sw.n += 16
        return i

    def wait_piece(eng, i):
        wait(eng, semW[i % 4], 16 * (i // 4 + 1))

    def piece_view(i, kt, m):
        off = (i % 2) * 4096
        return slotb[:, off:off + kt * m].rearrange("p (a b) -> p a b", a=kt, b=m)

    with nc.Block():
        for dst, src in [(gate_sb, d_gate), (b2_sb, d_b2), (b1_sb, d_b1),
                         (cw_sb, d_cw), (cb_sb, d_cb), (bdt_sb, d_bdt),
                         (D_sb, d_D), (A_sb, d_alog), (id_sb, d_id),
                         (oc_sb, d_oc), (or_sb, d_or), (sel_sb, d_sel)]:
            SY.dma_start(dst[:], src).then_inc(cCONST.h, 16); cCONST.n += 16
        GP.dma_start(wx_sb[:], d_wxdt[:, 0:1536]).then_inc(cPR.h, 16); cPR.n += 16
        GP.dma_start(wdt_sb[:], d_wxdt[0:64, 1536:3584]).then_inc(cPR.h, 16); cPR.n += 16
        for e in range(NE):
            for q in range(8):
                GP.dma_start(s_w1[e, q], d_w1[e, q]).then_inc(cPR.h, 16); cPR.n += 16
            for q in range(8):
                GP.dma_start(s_w2[e, q], d_w2[e, q]).then_inc(cPR.h, 16); cPR.n += 16
        for q in range(8):
            GP.dma_start(s_win[q], d_win[q]).then_inc(cPR.h, 16); cPR.n += 16
        for h in range(4):
            GP.dma_start(s_wout[h], d_wout[h]).then_inc(cPR.h, 16); cPR.n += 16
        prep_total = cPR.n

        wait(SE, cCONST, cCONST.n); wait(TE, cCONST, cCONST.n)
        wait(VE, cCONST, cCONST.n); wait(GP, cCONST, cCONST.n)
        i = SE.activation(sp_sb[:, 0:256], A_sb[:], AF.Exp)
        m = cSE.inc(i)
        wait(VE, cSE, m)
        VE.tensor_scalar(A_sb[:], sp_sb[:, 0:256], -1.0, None, ALU.mult)
        VE.memset(halo_sb[:], 0.0)
        VE.memset(hst_sb[:], 0.0)
        wait(SY, cPR, prep_total)

        def rs(c):
            return (c % 2) * 4096
        def xs(c):
            return (c % 2) * 4096

        def emit_ln(c, with_router):
            rb = rs(c)
            wait(SE, cTE, cTE.n); wait(SE, cVE, cVE.n); wait(SE, cGP, cGP.n)
            wait(TE, cGP, cGP.n); wait(TE, cVE, cVE.n)
            wait(SE, semO[0], semO[0].n); wait(SE, semO[1], semO[1].n)
            SE.drain(); VE.drain()
            sq_ms = {}
            for dt_ in range(4):
                i = SE.activation(hq[:, (dt_ % 4) * 512:(dt_ % 4) * 512 + 512],
                                  resid[:, rb + dt_ * 512:rb + dt_ * 512 + 512],
                                  AF.Square)
                sq_ms[dt_] = cSE.inc(i)
            te_take(0); te_take(1)
            for dt_ in range(NDT):
                i = TE.matmul(pb[0][0:1, 0:512], oc_sb[:],
                              resid[:, rb + dt_ * 512:rb + dt_ * 512 + 512],
                              start=(dt_ == 0), stop=(dt_ == NDT - 1))
            mm_ms = {}
            for dt_ in range(NDT):
                wait(TE, cSE, sq_ms[dt_])
                i = TE.matmul(pb[1][0:1, 0:512], oc_sb[:],
                              hq[:, (dt_ % 4) * 512:(dt_ % 4) * 512 + 512],
                              start=(dt_ == 0), stop=(dt_ == NDT - 1))
                mm_ms[dt_] = cTE.inc(i)
                nxt = dt_ + 4
                if nxt < NDT:
                    wait(SE, cTE, mm_ms[dt_])
                    if dt_ == 0:
                        SE.drain()
                    i = SE.activation(hq[:, (nxt % 4) * 512:(nxt % 4) * 512 + 512],
                                      resid[:, rb + nxt * 512:rb + nxt * 512 + 512],
                                      AF.Square)
                    sq_ms[nxt] = cSE.inc(i)
            mte = cTE.n
            wait(VE, cTE, mte)
            VE.tensor_scalar(ln_sb[0:1, 512:1024], pb[1][0:1, 0:512],
                             1.0 / D, None, ALU.mult)
            i = VE.tensor_scalar(ln_sb[0:1, 1024:1536], pb[0][0:1, 0:512],
                                 1.0 / D, None, ALU.mult)
            mve = cVE.inc(i)
            read_bank(0, cVE, mve); read_bank(1, cVE, mve)
            wait(SE, cVE, mve)
            SE.drain()
            i = SE.activation(ln_sb[0:1, 0:512], ln_sb[0:1, 1024:1536], AF.Square)
            mse = cSE.inc(i)
            wait(VE, cSE, mse)
            VE.drain()
            i = VE.scalar_tensor_tensor(bco_sb[0:1, 0:512], ln_sb[0:1, 512:1024],
                                        EPS, ln_sb[0:1, 0:512],
                                        ALU.add, ALU.subtract)
            mve = cVE.inc(i)
            wait(SE, cVE, mve)
            SE.activation(ln_sb[0:1, 512:1024], bco_sb[0:1, 0:512], AF.Ln)
            SE.drain()
            i = SE.activation(ln_sb[0:1, 0:512], ln_sb[0:1, 512:1024],
                              AF.Exp, scale=-0.5)
            mse = cSE.inc(i)
            wait(VE, cSE, mse)
            VE.drain()
            i = VE.scalar_tensor_tensor(bco_sb[0:1, 0:512], ln_sb[0:1, 1024:1536],
                                        -1.0, ln_sb[0:1, 0:512], ALU.mult, ALU.mult)
            mve = cVE.inc(i)
            te_take(0); te_take(1)
            wait(TE, cVE, mve)
            TE.matmul(pb[0][:, 0:512], or_sb[:], ln_sb[0:1, 0:512],
                      start=True, stop=True)
            i = TE.matmul(pb[1][:, 0:512], or_sb[:], bco_sb[0:1, 0:512],
                          start=True, stop=True)
            mte = cTE.inc(i)
            wait(VE, cTE, mte)
            VE.drain()
            for dt_ in range(NDT):
                VE.tensor_tensor(ysb[:, dt_ * 512:(dt_ + 1) * 512],
                                 resid[:, rb + dt_ * 512:rb + dt_ * 512 + 512],
                                 pb[0][:, 0:512], ALU.mult)
            VE.drain()
            if with_router:
                for dt_ in range(NDT):
                    i = VE.tensor_tensor(ysb[:, dt_ * 512:(dt_ + 1) * 512],
                                         pb[1][:, 0:512],
                                         ysb[:, dt_ * 512:(dt_ + 1) * 512], ALU.add)
                mve = cVE.inc(i)
                read_bank(0, cVE, mve); read_bank(1, cVE, mve)
                wait(SE, cVE, mve)
                SE.drain()
                for dt_ in range(NDT):
                    i = SE.activation(
                        xnbf[:, xs(c) + dt_ * 512:xs(c) + dt_ * 512 + 512],
                        ysb[:, dt_ * 512:(dt_ + 1) * 512], AF.Identity)
                cSE.inc(i)
                emit_router(c, mve)
            else:
                for dt_ in range(NDT):
                    i = VE.tensor_tensor(
                        xnbf[:, xs(c) + dt_ * 512:xs(c) + dt_ * 512 + 512],
                        ysb[:, dt_ * 512:(dt_ + 1) * 512],
                        pb[1][:, 0:512], ALU.add)
                mve = cVE.inc(i)
                read_bank(0, cVE, mve); read_bank(1, cVE, mve)

        def emit_router(c, m_xn):
            te_take(0)
            wait(TE, cVE, m_xn)
            for tt in range(4):
                for kt in range(NDT):
                    i = TE.matmul(pb[0][:, tt * 8:tt * 8 + 8],
                                  ysb[:, kt * 512 + tt * 128:kt * 512 + tt * 128 + 128],
                                  gate_sb[:, kt * 8:kt * 8 + 8],
                                  start=(kt == 0), stop=(kt == NDT - 1))
            mte = cTE.inc(i)
            wait(VE, cTE, mte)
            VE.drain()
            for tt in range(4):
                VE.reduce_max(rm_sb[:, tt:tt + 1], pb[0][:, tt * 8:tt * 8 + 8], axis=AX)
            VE.drain()
            for tt in range(4):
                i = VE.tensor_scalar(rm_sb[:, 4 + tt:5 + tt], rm_sb[:, tt:tt + 1],
                                     -1.0, None, ALU.mult)
            mve = cVE.inc(i)
            wait(SE, cVE, mve)
            SE.drain()
            for tt in range(4):
                i = SE.activation(rp_sb[:, tt * 8:tt * 8 + 8],
                                  pb[0][:, tt * 8:tt * 8 + 8],
                                  AF.Exp, bias=rm_sb[:, 4 + tt:5 + tt])
            mse = cSE.inc(i)
            wait(VE, cSE, mse)
            VE.drain()
            for tt in range(4):
                VE.reduce_sum(rm_sb[:, 8 + tt:9 + tt], rp_sb[:, tt * 8:tt * 8 + 8],
                              axis=AX)
            VE.drain()
            for tt in range(4):
                VE.reciprocal(rm_sb[:, 8 + tt:9 + tt], rm_sb[:, 8 + tt:9 + tt])
            VE.drain()
            for tt in range(4):
                VE.tensor_scalar(rm_sb[:, 8 + tt:9 + tt], rm_sb[:, 8 + tt:9 + tt],
                                 0.5, None, ALU.mult)
            VE.drain()
            for tt in range(4):
                i = VE.tensor_scalar(mw_sb[:, tt * 8:tt * 8 + 8],
                                     pb[0][:, tt * 8:tt * 8 + 8],
                                     rm_sb[:, tt:tt + 1], rm_sb[:, 8 + tt:9 + tt],
                                     ALU.is_equal, ALU.mult)
            mve = cVE.inc(i)
            read_bank(0, cVE, mve)
            te_take(1)
            wait(TE, cVE, mve)
            for tt in range(4):
                i = TE.transpose(pb[1][0:8, tt * 128:(tt + 1) * 128],
                                 mw_sb[:, tt * 8:tt * 8 + 8], id_sb[:])
            mte = cTE.inc(i)
            wait(SE, cTE, mte)
            SE.drain()
            i = SE.activation(mwt_sb[0:8, 0:512], pb[1][0:8, 0:512], AF.Identity)
            mse = cSE.inc(i)
            read_bank(1, cSE, mse)

        def emit_ffn(c, pump):
            rb = rs(c)
            xb = xs(c)
            m_mwt = cSE.n
            for j in range(NDT):
                bi = j % 2
                te_take(bi)
                wait(TE, cSE, m_mwt)
                i = TE.matmul(pb[bi][:, 0:512], b2_sb[0:8, j * 128:(j + 1) * 128],
                              mwt_sb[0:8, 0:512], start=True, stop=True)
                mte = cTE.inc(i)
                wait(VE, cTE, mte)
                if j < 2:
                    VE.drain()
                r = resid[:, rb + j * 512:rb + j * 512 + 512]
                i = VE.tensor_tensor(r, pb[bi][:, 0:512], r, ALU.add)
                mve = cVE.inc(i)
                read_bank(bi, cVE, mve)
            for e in range(NE):
                bi = 4 + e % 2
                te_take(bi)
                i = TE.matmul(pb[bi][:, 0:512], sel_sb[0:8, e * 128:(e + 1) * 128],
                              mwt_sb[0:8, 0:512], start=True, stop=True)
                m_mbc = cTE.inc(i)
                for p8 in range(8):
                    p1 = load_piece(s_w1[e, p8], 8, 512)
                    w1v = piece_view(p1, 8, 512)
                    wait_piece(TE, p1)
                    hslot = (p8 % 2) * 2048
                    for j in range(4):
                        hb = st["hy"] % 2; st["hy"] += 1
                        te_take(hb)
                        for kt in range(8):
                            i = TE.matmul(pb[hb][:, 0:512],
                                          w1v[:, kt, j * 128:(j + 1) * 128],
                                          xnbf[:, xb + kt * 512:xb + kt * 512 + 512],
                                          start=(kt == 0), stop=(kt == 7))
                        mte = cTE.inc(i)
                        if j == 3:
                            cWF.inc(TE.drain())
                        wait(SE, cTE, mte)
                        if j == 0:
                            SE.drain()
                        bcol = e * 32 + p8 * 4 + j
                        i = SE.activation(hqb[:, hslot + j * 512:hslot + j * 512 + 512],
                                          pb[hb][:, 0:512], AF.Gelu,
                                          bias=b1_sb[:, bcol:bcol + 1])
                        mse = cSE.inc(i)
                        read_bank(hb, cSE, mse)
                    m_h = cSE.n
                    pump(3)
                    p2 = load_piece(s_w2[e, p8], 4, 1024)
                    w2v = piece_view(p2, 4, 1024)
                    wait_piece(TE, p2)
                    wait(TE, cSE, m_h)
                    for j2 in range(NDT):
                        yb = st["hy"] % 2; st["hy"] += 1
                        te_take(yb)
                        for kt in range(4):
                            i = TE.matmul(pb[yb][:, 0:512],
                                          w2v[:, kt, j2 * 128:(j2 + 1) * 128],
                                          hqb[:, hslot + kt * 512:hslot + kt * 512 + 512],
                                          start=(kt == 0), stop=(kt == 3))
                        mte = cTE.inc(i)
                        if j2 == NDT - 1:
                            cWF.inc(TE.drain())
                        wait(VE, cTE, mte)
                        if j2 == 0:
                            VE.drain()
                        ys = ysb[:, j2 * 512:(j2 + 1) * 512]
                        if p8 == 0:
                            i = VE.tensor_copy(ys, pb[yb][:, 0:512])
                        else:
                            i = VE.tensor_tensor(ys, pb[yb][:, 0:512], ys, ALU.add)
                        mve = cVE.inc(i)
                        read_bank(yb, cVE, mve)
                    pump(3)
                wait(VE, cTE, m_mbc)
                VE.drain()
                for j in range(NDT):
                    sl = (j % 2) * 512
                    wait(VE, cGP, st["mt_slot"][j % 2])
                    VE.drain()
                    i = VE.tensor_tensor(mt_sb[:, sl:sl + 512],
                                         ysb[:, j * 512:(j + 1) * 512],
                                         pb[bi][:, 0:512], ALU.mult)
                    mve = cVE.inc(i)
                    wait(GP, cVE, mve)
                    r = resid[:, rb + j * 512:rb + j * 512 + 512]
                    i = GP.tensor_tensor(r, mt_sb[:, sl:sl + 512], r, ALU.add)
                    st["mt_slot"][j % 2] = cGP.inc(i)
                read_bank(bi, cVE, cVE.n)

        def xrw(g, a, b):
            return xr_t[:, g * XRS + a:g * XRS + b]
        def yaw(g):
            return ya_t[:, g * 256:(g + 1) * 256]
        def st1s(t):
            return st1_sb[:, (t % 3) * 256:(t % 3) * 256 + 256]

        def p67_take():
            bi = 6 + st["p67"] % 2; st["p67"] += 1
            te_take(bi)
            return bi

        def mamba_gen(c, m_xn_ve):
            rb = rs(c)
            xb = xs(c)
            for w in range(2):
                t0 = w * 256
                xz_ms = {}
                m_bprev = st.get("m_scanb")
                for pc in range(4):
                    p = load_piece(s_win[pc], 8, 512)
                    wv = piece_view(p, 8, 512)
                    wait_piece(TE, p)
                    wait(TE, cVE, m_xn_ve)
                    for mt in range(4):
                        bi = p67_take()
                        for kt in range(8):
                            i = TE.matmul(
                                pb[bi][:, 0:256], wv[:, kt, mt * 128:(mt + 1) * 128],
                                xnbf[:, xb + kt * 512 + t0:xb + kt * 512 + t0 + 256],
                                start=(kt == 0), stop=(kt == 7))
                        mte = cTE.inc(i)
                        if mt == 3:
                            cWF.inc(TE.drain())
                        wait(SE, cTE, mte)
                        if pc == 0 and mt == 0:
                            wait(SE, cVE, m_bprev)
                            SE.drain()
                        gi = pc * 4 + mt
                        i = SE.activation(xrw(gi, 3, 259), pb[bi][:, 0:256],
                                          AF.Identity)
                        xz_ms[gi] = cSE.inc(i)
                        read_bank(bi, cSE, xz_ms[gi])
                    yield
                wait(VE, cSE, xz_ms[NG - 1])
                VE.drain()
                xr_g = xr_t[:, 0:NG * XRS].rearrange("p (g a) -> p g a", g=NG, a=XRS)
                halo_g = halo_sb[:].rearrange("p (g a) -> p g a", g=NG, a=3)
                VE.tensor_copy(xr_g[:, :, 0:3], halo_g)
                VE.drain()
                VE.tensor_copy(halo_g, xr_g[:, :, 256:259])
                VE.drain()
                silu_ms = {}
                for g0 in range(0, NG, 2):
                    if g0 >= 2:
                        wait(VE, cSE, silu_ms[g0 - 1])
                        VE.drain()
                    for g in (g0, g0 + 1):
                        sl = (g % 2) * 256
                        VE.tensor_scalar(cv_sb[:, sl:sl + 256], xrw(g, 0, 256),
                                         cw_sb[:, g * 4:g * 4 + 1], None, ALU.mult)
                    for k in range(1, 4):
                        VE.drain()
                        for g in (g0, g0 + 1):
                            sl = (g % 2) * 256
                            i = VE.scalar_tensor_tensor(
                                cv_sb[:, sl:sl + 256], xrw(g, k, 256 + k),
                                cw_sb[:, g * 4 + k:g * 4 + k + 1],
                                cv_sb[:, sl:sl + 256], ALU.mult, ALU.add)
                    mve = cVE.inc(i)
                    wait(SE, cVE, mve)
                    if g0 == 0:
                        SE.drain()
                    for g in (g0, g0 + 1):
                        sl = (g % 2) * 256
                        i = SE.activation(xrw(g, 3, 259), cv_sb[:, sl:sl + 256],
                                          AF.Silu, bias=cb_sb[:, g:g + 1])
                        silu_ms[g] = cSE.inc(i)
                    if g0 % 4 == 2:
                        yield
                SE.drain()
                for g in range(NG):
                    i = SE.activation(ybf_t[:, g * 256:(g + 1) * 256],
                                      xrw(g, 3, 259), AF.Identity)
                m_xbf = cSE.inc(i)
                yield
                te_take(6); te_take(7)
                wait(TE, cSE, m_xbf)
                for kt in range(NG):
                    TE.matmul(pb[6][0:64, 0:256], wx_sb[:, kt * 96:kt * 96 + 64],
                              ybf_t[:, kt * 256:(kt + 1) * 256],
                              start=(kt == 0), stop=(kt == NG - 1))
                for kt in range(NG):
                    TE.matmul(pb[7][0:16, 0:256], wx_sb[:, kt * 96 + 64:kt * 96 + 80],
                              ybf_t[:, kt * 256:(kt + 1) * 256],
                              start=(kt == 0), stop=(kt == NG - 1))
                for kt in range(NG):
                    i = TE.matmul(pb[7][0:16, 256:512],
                                  wx_sb[:, kt * 96 + 80:kt * 96 + 96],
                                  ybf_t[:, kt * 256:(kt + 1) * 256],
                                  start=(kt == 0), stop=(kt == NG - 1))
                mte = cTE.inc(i)
                wait(SE, cTE, mte)
                SE.drain()
                SE.activation(dl_sb[:], pb[6][0:64, 0:256], AF.Identity)
                i = SE.activation(bc_sb[0:16, 0:512], pb[7][0:16, 0:512], AF.Identity)
                m_bcs = cSE.inc(i)
                read_bank(6, cSE, m_bcs); read_bank(7, cSE, m_bcs)
                yield
                dtv_ms = {}
                for mt0 in range(0, NG, 2):
                    bis = []
                    for mt in (mt0, mt0 + 1):
                        bi = p67_take()
                        wait(TE, cSE, m_bcs)
                        i = TE.matmul(pb[bi][0:128, 0:256],
                                      wdt_sb[:, mt * 128:(mt + 1) * 128],
                                      dl_sb[:], start=True, stop=True)
                        bis.append(bi)
                    mte = cTE.inc(i)
                    wait(SE, cTE, mte)
                    if mt0 >= 2:
                        wait(SE, cVE, dtv_ms[mt0 - 1])
                    SE.drain()
                    for k, mt in enumerate((mt0, mt0 + 1)):
                        SE.activation(sp_sb[:, k * 256:k * 256 + 256],
                                      pb[bis[k]][:, 0:256], AF.Abs,
                                      bias=bdt_sb[:, mt:mt + 1])
                    SE.drain()
                    for k in range(2):
                        SE.activation(sp_sb[:, 512 + k * 256:512 + k * 256 + 256],
                                      sp_sb[:, k * 256:k * 256 + 256],
                                      AF.Exp, scale=-1.0)
                    SE.drain()
                    for k in range(2):
                        SE.activation(sp_sb[:, k * 256:k * 256 + 256],
                                      sp_sb[:, 512 + k * 256:512 + k * 256 + 256],
                                      AF.Ln, bias=1.0)
                    SE.drain()
                    for k, mt in enumerate((mt0, mt0 + 1)):
                        i = SE.activation(dt_t[:, mt * 256:(mt + 1) * 256],
                                          pb[bis[k]][:, 0:256],
                                          AF.Relu, bias=bdt_sb[:, mt:mt + 1])
                    mse = cSE.inc(i)
                    read_bank(bis[0], cSE, mse); read_bank(bis[1], cSE, mse)
                    wait(VE, cSE, mse)
                    VE.drain()
                    for k, mt in enumerate((mt0, mt0 + 1)):
                        i = VE.tensor_tensor(dt_t[:, mt * 256:(mt + 1) * 256],
                                             sp_sb[:, k * 256:k * 256 + 256],
                                             dt_t[:, mt * 256:(mt + 1) * 256],
                                             ALU.add)
                        dtv_ms[mt] = cVE.inc(i)
                    if mt0 % 4 == 2:
                        yield
                VE.drain()
                for g0 in range(0, NG, 2):
                    for g in (g0, g0 + 1):
                        VE.tensor_scalar(yaw(g), xrw(g, 3, 259), D_sb[:, g:g + 1],
                                         None, ALU.mult)
                    VE.drain()
                    for g in (g0, g0 + 1):
                        i = VE.tensor_tensor(xrw(g, 3, 259),
                                             dt_t[:, g * 256:(g + 1) * 256],
                                             xrw(g, 3, 259), ALU.mult)
                m_u = cVE.inc(i)
                VE.drain()
                yield
                base_gp = cGP.n
                scan_ms = {}
                gp_ms = {}
                pend = []
                if True:
                    GP.drain()
                for s in range(DS):
                    bB = 6 + (s % 2)
                    te_take(bB)
                    TE.matmul(pb[bB][:, 0:256], sel_sb[0:16, s * 128:(s + 1) * 128],
                              bc_sb[0:16, 0:256], start=True, stop=True)
                    i = TE.matmul(pb[bB][:, 256:512],
                                  sel_sb[0:16, s * 128:(s + 1) * 128],
                                  bc_sb[0:16, 256:512], start=True, stop=True)
                    mte = cTE.inc(i)
                    csl = (s % 2) * 256
                    if s >= 2:
                        wait(SE, cGP, gp_ms[(s - 2) * NG + (NG - 1)])
                    wait(SE, cTE, mte)
                    SE.drain()
                    i = SE.activation(cc_sb[:, csl:csl + 256], pb[bB][:, 256:512],
                                      AF.Identity)
                    msec = cSE.inc(i)
                    read_bank(bB, cSE, msec)
                    for g in range(NG):
                        t = s * NG + g
                        asl = (g % 2) * 256
                        wait(SE, cVE, dtv_ms[g])
                        if t >= 2:
                            wait(SE, cVE, scan_ms[t - 2])
                        SE.drain()
                        i = SE.activation(sa_sb[:, asl:asl + 256],
                                          dt_t[:, g * 256:(g + 1) * 256], AF.Exp,
                                          scale=A_sb[:, g * 16 + s:g * 16 + s + 1])
                        msea = cSE.inc(i)
                        wait(VE, cSE, msea)
                        wait(VE, cTE, mte)
                        ib = VE.tensor_tensor(sb1_sb[:, (t % 2) * 256:(t % 2) * 256 + 256],
                                              xrw(g, 3, 259), pb[bB][:, 0:256],
                                              ALU.mult)
                        mvb = cVE.inc(ib)
                        for lag in list(pend):
                            if lag[0] <= t - 2:
                                pend.remove(lag)
                                wait(VE, cGP, gp_ms[lag[0]])
                                i2 = VE.tensor_tensor(yaw(lag[1]), st1s(lag[0]),
                                                      yaw(lag[1]), ALU.add)
                                cVE.inc(i2)
                        wait(VE, cVE, mvb)
                        hsl = (g % 2) * 256
                        if t >= 2:
                            wait(VE, cGP, gp_ms[t - 2])
                        i = VE.tensor_tensor_scan(
                            sh_sb[:, hsl:hsl + 256], sa_sb[:, asl:asl + 256],
                            sb1_sb[:, (t % 2) * 256:(t % 2) * 256 + 256],
                            hst_sb[:, g * 16 + s:g * 16 + s + 1],
                            ALU.mult, ALU.add)
                        scan_ms[t] = cVE.inc(i)
                        wait(GP, cVE, scan_ms[t])
                        wait(GP, cSE, msec)
                        GP.tensor_tensor(st1s(t), sh_sb[:, hsl:hsl + 256],
                                         cc_sb[:, csl:csl + 256], ALU.mult)
                        i = GP.tensor_copy(hst_sb[:, g * 16 + s:g * 16 + s + 1],
                                           sh_sb[:, hsl + 255:hsl + 256])
                        gp_ms[t] = cGP.inc(i)
                        pend.append((t, g))
                        if g % 4 == 3:
                            yield
                    read_bank(bB, cVE, cVE.n)
                st["m_scanb"] = cVE.n
                for lag in pend:
                    wait(VE, cGP, gp_ms[lag[0]])
                    i = VE.tensor_tensor(yaw(lag[1]), st1s(lag[0]),
                                         yaw(lag[1]), ALU.add)
                    cVE.inc(i)
                m_scan = cVE.n
                yield
                gt_ms = {}
                for pc in range(4):
                    p = load_piece(s_win[4 + pc], 8, 512)
                    wv = piece_view(p, 8, 512)
                    wait_piece(TE, p)
                    for mt in range(4):
                        bi = p67_take()
                        for kt in range(8):
                            i = TE.matmul(
                                pb[bi][:, 0:256], wv[:, kt, mt * 128:(mt + 1) * 128],
                                xnbf[:, xb + kt * 512 + t0:xb + kt * 512 + t0 + 256],
                                start=(kt == 0), stop=(kt == 7))
                        mte = cTE.inc(i)
                        if mt == 3:
                            cWF.inc(TE.drain())
                        gi = pc * 4 + mt
                        wait(SE, cTE, mte)
                        if gi >= 2:
                            wait(SE, cVE, gt_ms[gi - 2])
                        SE.drain()
                        sl = (gi % 2) * 256
                        i = SE.activation(gz_sb[:, sl:sl + 256], pb[bi][:, 0:256],
                                          AF.Silu)
                        mse = cSE.inc(i)
                        read_bank(bi, cSE, mse)
                        wait(VE, cSE, mse)
                        if gi == 0:
                            wait(VE, cVE, m_scan)
                            VE.drain()
                        i = VE.tensor_tensor(ybf_t[:, gi * 256:(gi + 1) * 256],
                                             yaw(gi), gz_sb[:, sl:sl + 256],
                                             ALU.mult)
                        gt_ms[gi] = cVE.inc(i)
                    yield
                m_ybf = cVE.n
                for pc in range(4):
                    p = load_piece(s_wout[pc], 16, 256)
                    wv = piece_view(p, 16, 256)
                    wait_piece(TE, p)
                    wait(TE, cVE, m_ybf)
                    for mt in range(2):
                        bi = p67_take()
                        for kt in range(NG):
                            i = TE.matmul(pb[bi][:, 0:256],
                                          wv[:, kt, mt * 128:(mt + 1) * 128],
                                          ybf_t[:, kt * 256:(kt + 1) * 256],
                                          start=(kt == 0), stop=(kt == NG - 1))
                        mte = cTE.inc(i)
                        if mt == 1:
                            cWF.inc(TE.drain())
                        wait(VE, cTE, mte)
                        dt_i = pc * 2 + mt
                        if dt_i == 0:
                            VE.drain()
                        r = resid[:, rb + dt_i * 512 + t0:rb + dt_i * 512 + t0 + 256]
                        i = VE.tensor_tensor(r, pb[bi][:, 0:256], r, ALU.add)
                        mve = cVE.inc(i)
                        read_bank(bi, cVE, mve)
                    if pc % 2 == 1:
                        yield

        def emit_in(c):
            wait(SY, semO[0], semO[0].n)
            wait(SY, semO[1], semO[1].n)
            wait(SY, cSE, cSE.n)
            wait(SY, cTE, cTE.n)
            rb = rs(c)
            for tt in range(4):
                g = c * 4 + tt
                wait(SY, cSE, st["stage_in"][tt % 2])
                si = semI[g % 2]
                SY.dma_start(hq[:, (tt % 2) * 1024:(tt % 2) * 1024 + 1024],
                             d_x[g * 128:(g + 1) * 128, :]).then_inc(si.h, 16)
                si.n += 16
                wait(TE, si, si.n)
                for dt_ in range(NDT):
                    bi = dt_ % 2
                    te_take(bi)
                    i = TE.transpose(pb[bi][:, 0:128],
                                     hq[:, (tt % 2) * 1024 + dt_ * 128:
                                         (tt % 2) * 1024 + dt_ * 128 + 128],
                                     id_sb[:])
                    mte = cTE.inc(i)
                    wait(SE, cTE, mte)
                    if dt_ == 0:
                        SE.drain()
                    i = SE.activation(
                        resid[:, rb + dt_ * 512 + tt * 128:rb + dt_ * 512 + tt * 128 + 128],
                        pb[bi][:, 0:128], AF.Identity)
                    mse = cSE.inc(i)
                    read_bank(bi, cSE, mse)
                st["stage_in"][tt % 2] = mse

        def emit_out(c):
            rb = rs(c)
            wait(TE, cGP, cGP.n)
            wait(SE, cGP, cGP.n)
            for tt in range(4):
                g = c * 4 + tt
                so = semO[g % 2]
                for dt_ in range(NDT):
                    bi = dt_ % 2
                    te_take(bi)
                    i = TE.transpose(
                        pb[bi][:, 0:128],
                        resid[:, rb + dt_ * 512 + tt * 128:rb + dt_ * 512 + tt * 128 + 128],
                        id_sb[:])
                    mte = cTE.inc(i)
                    wait(SE, cTE, mte)
                    if dt_ == 0:
                        wait(SE, so, so.n)
                        SE.drain()
                    i = SE.activation(hq[:, (tt % 2) * 1024 + dt_ * 128:
                                          (tt % 2) * 1024 + dt_ * 128 + 128],
                                      pb[bi][:, 0:128], AF.Identity)
                    mse = cSE.inc(i)
                    read_bank(bi, cSE, mse)
                wait(SY, cSE, mse)
                SY.dma_start(d_out[g * 128:(g + 1) * 128, :],
                             hq[:, (tt % 2) * 1024:(tt % 2) * 1024 + 1024]
                             ).then_inc(so.h, 16)
                so.n += 16

        st["gen"] = None
        def pump(k):
            g = st["gen"]
            if g is None:
                return
            for _ in range(k):
                try:
                    next(g)
                except StopIteration:
                    st["gen"] = None
                    return
        def flush():
            g = st["gen"]
            if g is not None:
                for _ in g:
                    pass
                st["gen"] = None

        for c in range(nch):
            emit_in(c)
            emit_ln(c, True)
            emit_ffn(c, pump)
            emit_ln(c, False)
            flush()
            st["gen"] = mamba_gen(c, cVE.n)
            if c >= 1:
                j = c - 1
                emit_ln(j, True)
                emit_ffn(j, pump)
                emit_out(j)
        flush()
        emit_ln(nch - 1, True)
        emit_ffn(nch - 1, lambda k: None)
        emit_out(nch - 1)
        SY.wait_ge(semO[0].h, semO[0].n)
        SY.wait_ge(semO[1].h, semO[1].n)

    return nc


def _prearrange(inputs):
    w1 = inputs["L0_ffn_w1"]
    w2 = inputs["L0_ffn_w2"]
    win = inputs["L0_attn_W_in"]
    wout = inputs["L0_attn_W_out"]
    wx = inputs["L0_attn_W_x"]
    wdt = inputs["L0_attn_W_dt"]
    gate = inputs["L0_ffn_gate_w"]
    b1 = inputs["L0_ffn_b1"]
    b2 = inputs["L0_ffn_b2"]
    cw = inputs["L0_attn_conv_w"]
    cb = inputs["L0_attn_conv_b"]
    bdt = inputs["L0_attn_b_dt"]
    Dv = inputs["L0_attn_D"]
    alog = inputs["L0_attn_A_log"]

    d = {}
    d["w1s"] = np.ascontiguousarray(
        w1.reshape(8, 8, 128, 8, 512).transpose(0, 3, 1, 2, 4))
    d["w2s"] = np.ascontiguousarray(w2.reshape(8, 8, 4, 128, 1024))
    d["wins"] = np.ascontiguousarray(
        win.reshape(8, 128, 8, 512).transpose(2, 0, 1, 3))
    d["wouts"] = np.ascontiguousarray(
        wout.reshape(16, 128, 4, 256).transpose(2, 0, 1, 3))
    wxdt = np.zeros((128, 3584), dtype=np.float32)
    wxdt[:, 0:1536] = wx.reshape(16, 128, 96).transpose(1, 0, 2).reshape(128, 1536)
    wxdt[0:64, 1536:3584] = wdt
    d["wxdt"] = wxdt
    d["gates"] = np.ascontiguousarray(
        gate.reshape(8, 128, 8).transpose(1, 0, 2).reshape(128, 64))
    d["b2s"] = np.ascontiguousarray(b2)
    d["b1s"] = np.ascontiguousarray(
        b1.reshape(8, 8, 4, 128).transpose(3, 0, 1, 2).reshape(128, 256))
    d["convw"] = np.ascontiguousarray(
        cw.T.reshape(16, 128, 4).transpose(1, 0, 2).reshape(128, 64))
    d["convb"] = np.ascontiguousarray(cb.reshape(16, 128).T)
    d["bdts"] = np.ascontiguousarray(bdt.reshape(16, 128).T)
    d["Dg"] = np.ascontiguousarray(Dv.reshape(16, 128).T)
    d["alog"] = np.ascontiguousarray(
        alog.reshape(16, 128, 16).transpose(1, 0, 2).reshape(128, 256))
    d["ident"] = np.eye(128, dtype=np.float32)
    d["onesc"] = np.ones((128, 1), dtype=np.float32)
    d["onesr"] = np.ones((1, 128), dtype=np.float32)
    sel = np.zeros((16, 2048), dtype=np.float32)
    for s in range(16):
        sel[s, s * 128:(s + 1) * 128] = 1.0
    d["sel16"] = sel
    return d


def kernel(**inputs):
    x = inputs["L0_attn_hidden_states"]
    shared = _prearrange(inputs)
    nc = build_nc()
    in_maps = []
    for core in range(8):
        m = dict(shared)
        m["x"] = np.ascontiguousarray(x[core])
        in_maps.append(m)
    res = run_bass_kernel_spmd(nc, in_maps, core_ids=list(range(8)))
    out = np.stack([np.asarray(res.results[i]["out"]) for i in range(8)], axis=0)
    return out.astype(np.float32, copy=False)


# version 06
# speedup vs baseline: 1.0225x; 1.0089x over previous
import numpy as np
import concourse.bass as bass
import concourse.mybir as mybir
from concourse.bass_utils import run_bass_kernel_spmd

F32 = mybir.dt.float32
BF16 = mybir.dt.bfloat16
AF = mybir.ActivationFunctionType
ALU = mybir.AluOpType
AX = mybir.AxisListType.X

L = 8192
D = 1024
DI = 2048
NE = 8
DS = 16
NCH = 16
C = 512
TW = 256
NDT = 8
NG = 16
EPS = 1e-5
XRS = 260


def build_nc(nch=NCH):
    nc = bass.Bass("TRN2", target_bir_lowering=False)

    d_x = nc.dram_tensor("x", [L, D], F32, kind="ExternalInput").ap()
    d_w1 = nc.dram_tensor("w1s", [NE, 8, 8, 128, 512], F32, kind="ExternalInput").ap()
    d_w2 = nc.dram_tensor("w2s", [NE, 8, 4, 128, 1024], F32, kind="ExternalInput").ap()
    d_win = nc.dram_tensor("wins", [8, 8, 128, 512], F32, kind="ExternalInput").ap()
    d_wout = nc.dram_tensor("wouts", [4, 16, 128, 256], F32, kind="ExternalInput").ap()
    d_wxdt = nc.dram_tensor("wxdt", [128, 3584], F32, kind="ExternalInput").ap()
    d_gate = nc.dram_tensor("gates", [128, 64], F32, kind="ExternalInput").ap()
    d_b2 = nc.dram_tensor("b2s", [8, 1024], F32, kind="ExternalInput").ap()
    d_b1 = nc.dram_tensor("b1s", [128, 256], F32, kind="ExternalInput").ap()
    d_cw = nc.dram_tensor("convw", [128, 64], F32, kind="ExternalInput").ap()
    d_cb = nc.dram_tensor("convb", [128, 16], F32, kind="ExternalInput").ap()
    d_bdt = nc.dram_tensor("bdts", [128, 16], F32, kind="ExternalInput").ap()
    d_D = nc.dram_tensor("Dg", [128, 16], F32, kind="ExternalInput").ap()
    d_alog = nc.dram_tensor("alog", [128, 256], F32, kind="ExternalInput").ap()
    d_id = nc.dram_tensor("ident", [128, 128], F32, kind="ExternalInput").ap()
    d_oc = nc.dram_tensor("onesc", [128, 1], F32, kind="ExternalInput").ap()
    d_or = nc.dram_tensor("onesr", [1, 128], F32, kind="ExternalInput").ap()
    d_sel = nc.dram_tensor("sel16", [16, 2048], F32, kind="ExternalInput").ap()
    d_out = nc.dram_tensor("out", [L, D], F32, kind="ExternalOutput").ap()

    s_w1 = nc.dram_tensor("w1b", [NE, 8, 8, 128, 512], BF16, kind="Internal").ap()
    s_w2 = nc.dram_tensor("w2b", [NE, 8, 4, 128, 1024], BF16, kind="Internal").ap()
    s_win = nc.dram_tensor("winb", [8, 8, 128, 512], BF16, kind="Internal").ap()
    s_wout = nc.dram_tensor("woutb", [4, 16, 128, 256], BF16, kind="Internal").ap()

    resid = nc.alloc_sbuf_tensor("resid", [128, 8192], F32)
    xnbf = nc.alloc_sbuf_tensor("xnbf", [128, 8192], BF16)
    slots = nc.alloc_sbuf_tensor("slots", [128, 4096], F32)
    slotb = slots[:].bitcast(BF16)
    ysb = nc.alloc_sbuf_tensor("ysb", [128, 4096], F32)
    hq = nc.alloc_sbuf_tensor("hq", [128, 2048], F32)
    hqb = hq[:].bitcast(BF16)
    xr_t = nc.alloc_sbuf_tensor("xr_t", [128, NG * XRS], F32)
    dt_t = nc.alloc_sbuf_tensor("dt_t", [128, 4096], F32)
    ya_t = nc.alloc_sbuf_tensor("ya_t", [128, 4096], F32)
    ybf_t = nc.alloc_sbuf_tensor("ybf_t", [128, 4096], BF16)
    sa_sb = nc.alloc_sbuf_tensor("sa_sb", [128, 768], F32)
    sh_sb = nc.alloc_sbuf_tensor("sh_sb", [128, 512], F32)
    sb1_sb = nc.alloc_sbuf_tensor("sb1_sb", [128, 768], F32)
    st1_sb = nc.alloc_sbuf_tensor("st1_sb", [128, 768], F32)
    cc_sb = nc.alloc_sbuf_tensor("cc_sb", [128, 1024], F32)
    bc_sb = nc.alloc_sbuf_tensor("bc_sb", [16, 512], F32)
    dl_sb = nc.alloc_sbuf_tensor("dl_sb", [64, 256], BF16)
    gz_sb = nc.alloc_sbuf_tensor("gz_sb", [128, 512], F32)
    cv_sb = nc.alloc_sbuf_tensor("cv_sb", [128, 512], F32)
    sp_sb = nc.alloc_sbuf_tensor("sp_sb", [128, 1024], F32)
    mt_sb = nc.alloc_sbuf_tensor("mt_sb", [128, 1024], F32)
    ln_sb = nc.alloc_sbuf_tensor("ln_sb", [1, 1536], F32)
    bco_sb = nc.alloc_sbuf_tensor("bco_sb", [1, 512], F32)
    rp_sb = nc.alloc_sbuf_tensor("rp_sb", [128, 32], F32)
    rm_sb = nc.alloc_sbuf_tensor("rm_sb", [128, 12], F32)
    mw_sb = mt_sb
    mwt_sb = nc.alloc_sbuf_tensor("mwt_sb", [8, 512], F32)
    wx_sb = nc.alloc_sbuf_tensor("wx_sb", [128, 1536], BF16)
    wdt_sb = nc.alloc_sbuf_tensor("wdt_sb", [64, 2048], BF16)
    gate_sb = nc.alloc_sbuf_tensor("gate_sb", [128, 64], F32)
    b2_sb = nc.alloc_sbuf_tensor("b2_sb", [8, 1024], F32)
    b1_sb = nc.alloc_sbuf_tensor("b1_sb", [128, 256], F32)
    cw_sb = nc.alloc_sbuf_tensor("cw_sb", [128, 64], F32)
    cb_sb = nc.alloc_sbuf_tensor("cb_sb", [128, 16], F32)
    bdt_sb = nc.alloc_sbuf_tensor("bdt_sb", [128, 16], F32)
    D_sb = nc.alloc_sbuf_tensor("D_sb", [128, 16], F32)
    A_sb = nc.alloc_sbuf_tensor("A_sb", [128, 256], F32)
    id_sb = nc.alloc_sbuf_tensor("id_sb", [128, 128], F32)
    oc_sb = nc.alloc_sbuf_tensor("oc_sb", [128, 1], F32)
    or_sb = nc.alloc_sbuf_tensor("or_sb", [1, 128], F32)
    sel_sb = nc.alloc_sbuf_tensor("sel_sb", [16, 2048], F32)
    hst_sb = nc.alloc_sbuf_tensor("hst_sb", [128, 256], F32)
    halo_sb = nc.alloc_sbuf_tensor("halo_sb", [128, 48], F32)

    pb = [nc.alloc_psum_tensor(f"pb{i}", [128, 512], F32) for i in range(8)]

    TE, SE, VE, GP, SY = nc.tensor, nc.scalar, nc.vector, nc.gpsimd, nc.sync

    class Ctr:
        __slots__ = ("h", "n")
        def __init__(self, name):
            self.h = nc.alloc_semaphore(name)
            self.n = 0
        def inc(self, instr, k=1):
            instr.then_inc(self.h, k)
            self.n += k
            return self.n

    cCONST = Ctr("cCONST"); cPR = Ctr("cPR")
    semW = [Ctr(f"semW{i}") for i in range(4)]
    semI = [Ctr(f"semI{i}") for i in range(2)]
    semO = [Ctr(f"semO{i}") for i in range(2)]
    cWF = Ctr("cWF")
    cTE = Ctr("cTE"); cSE = Ctr("cSE"); cVE = Ctr("cVE"); cGP = Ctr("cGP")

    waited = {}
    def wait(eng, ctr, n):
        if n is None or n <= 0:
            return
        key = (id(eng), id(ctr))
        if waited.get(key, -1) >= n:
            return
        waited[key] = n
        eng.wait_ge(ctr.h, n)

    bank_rd = [[] for _ in range(8)]
    def te_take(bi):
        for ctr, n in bank_rd[bi]:
            wait(TE, ctr, n)
        bank_rd[bi] = []
    def read_bank(bi, ctr, n):
        bank_rd[bi].append((ctr, n))

    st = {"piece": 0, "hy": 0, "p67": 0,
          "mt_slot": [None, None], "stage_in": [None, None]}

    def load_piece(src3d, kt, m):
        i = st["piece"]; st["piece"] += 1
        off = (i % 2) * 4096
        wait(SY, cWF, i - 1)
        dst = slotb[:, off:off + kt * m].rearrange("p (a b) -> p a b", a=kt, b=m)
        sw = semW[i % 4]
        SY.dma_start(dst, src3d.rearrange("a p b -> p a b")).then_inc(sw.h, 16)
        sw.n += 16
        return i

    def wait_piece(eng, i):
        wait(eng, semW[i % 4], 16 * (i // 4 + 1))

    def piece_view(i, kt, m):
        off = (i % 2) * 4096
        return slotb[:, off:off + kt * m].rearrange("p (a b) -> p a b", a=kt, b=m)

    with nc.Block():
        for dst, src in [(gate_sb, d_gate), (b2_sb, d_b2), (b1_sb, d_b1),
                         (cw_sb, d_cw), (cb_sb, d_cb), (bdt_sb, d_bdt),
                         (D_sb, d_D), (A_sb, d_alog), (id_sb, d_id),
                         (oc_sb, d_oc), (or_sb, d_or), (sel_sb, d_sel)]:
            SY.dma_start(dst[:], src).then_inc(cCONST.h, 16); cCONST.n += 16
        GP.dma_start(wx_sb[:], d_wxdt[:, 0:1536]).then_inc(cPR.h, 16); cPR.n += 16
        GP.dma_start(wdt_sb[:], d_wxdt[0:64, 1536:3584]).then_inc(cPR.h, 16); cPR.n += 16
        for e in range(NE):
            for q in range(8):
                GP.dma_start(s_w1[e, q], d_w1[e, q]).then_inc(cPR.h, 16); cPR.n += 16
            for q in range(8):
                GP.dma_start(s_w2[e, q], d_w2[e, q]).then_inc(cPR.h, 16); cPR.n += 16
        for q in range(8):
            GP.dma_start(s_win[q], d_win[q]).then_inc(cPR.h, 16); cPR.n += 16
        for h in range(4):
            GP.dma_start(s_wout[h], d_wout[h]).then_inc(cPR.h, 16); cPR.n += 16
        prep_total = cPR.n

        wait(SE, cCONST, cCONST.n); wait(TE, cCONST, cCONST.n)
        wait(VE, cCONST, cCONST.n); wait(GP, cCONST, cCONST.n)
        i = SE.activation(sp_sb[:, 0:256], A_sb[:], AF.Exp)
        m = cSE.inc(i)
        wait(VE, cSE, m)
        VE.tensor_scalar(A_sb[:], sp_sb[:, 0:256], -1.0, None, ALU.mult)
        VE.memset(halo_sb[:], 0.0)
        VE.memset(hst_sb[:], 0.0)
        wait(SY, cPR, prep_total)

        def rs(c):
            return (c % 2) * 4096
        def xs(c):
            return (c % 2) * 4096

        def emit_ln(c, with_router):
            rb = rs(c)
            wait(SE, cTE, cTE.n); wait(SE, cVE, cVE.n); wait(SE, cGP, cGP.n)
            wait(TE, cGP, cGP.n); wait(TE, cVE, cVE.n)
            wait(SE, semO[0], semO[0].n); wait(SE, semO[1], semO[1].n)
            SE.drain(); VE.drain()
            sq_ms = {}
            for dt_ in range(4):
                i = SE.activation(hq[:, (dt_ % 4) * 512:(dt_ % 4) * 512 + 512],
                                  resid[:, rb + dt_ * 512:rb + dt_ * 512 + 512],
                                  AF.Square)
                sq_ms[dt_] = cSE.inc(i)
            te_take(0); te_take(1)
            for dt_ in range(NDT):
                i = TE.matmul(pb[0][0:1, 0:512], oc_sb[:],
                              resid[:, rb + dt_ * 512:rb + dt_ * 512 + 512],
                              start=(dt_ == 0), stop=(dt_ == NDT - 1))
            mm_ms = {}
            for dt_ in range(NDT):
                wait(TE, cSE, sq_ms[dt_])
                i = TE.matmul(pb[1][0:1, 0:512], oc_sb[:],
                              hq[:, (dt_ % 4) * 512:(dt_ % 4) * 512 + 512],
                              start=(dt_ == 0), stop=(dt_ == NDT - 1))
                mm_ms[dt_] = cTE.inc(i)
                nxt = dt_ + 4
                if nxt < NDT:
                    wait(SE, cTE, mm_ms[dt_])
                    if dt_ == 0:
                        SE.drain()
                    i = SE.activation(hq[:, (nxt % 4) * 512:(nxt % 4) * 512 + 512],
                                      resid[:, rb + nxt * 512:rb + nxt * 512 + 512],
                                      AF.Square)
                    sq_ms[nxt] = cSE.inc(i)
            mte = cTE.n
            wait(VE, cTE, mte)
            VE.tensor_scalar(ln_sb[0:1, 512:1024], pb[1][0:1, 0:512],
                             1.0 / D, None, ALU.mult)
            i = VE.tensor_scalar(ln_sb[0:1, 1024:1536], pb[0][0:1, 0:512],
                                 1.0 / D, None, ALU.mult)
            mve = cVE.inc(i)
            read_bank(0, cVE, mve); read_bank(1, cVE, mve)
            wait(SE, cVE, mve)
            SE.drain()
            i = SE.activation(ln_sb[0:1, 0:512], ln_sb[0:1, 1024:1536], AF.Square)
            mse = cSE.inc(i)
            wait(VE, cSE, mse)
            VE.drain()
            i = VE.scalar_tensor_tensor(bco_sb[0:1, 0:512], ln_sb[0:1, 512:1024],
                                        EPS, ln_sb[0:1, 0:512],
                                        ALU.add, ALU.subtract)
            mve = cVE.inc(i)
            wait(SE, cVE, mve)
            SE.activation(ln_sb[0:1, 512:1024], bco_sb[0:1, 0:512], AF.Ln)
            SE.drain()
            i = SE.activation(ln_sb[0:1, 0:512], ln_sb[0:1, 512:1024],
                              AF.Exp, scale=-0.5)
            mse = cSE.inc(i)
            wait(VE, cSE, mse)
            VE.drain()
            i = VE.scalar_tensor_tensor(bco_sb[0:1, 0:512], ln_sb[0:1, 1024:1536],
                                        -1.0, ln_sb[0:1, 0:512], ALU.mult, ALU.mult)
            mve = cVE.inc(i)
            te_take(0); te_take(1)
            wait(TE, cVE, mve)
            TE.matmul(pb[0][:, 0:512], or_sb[:], ln_sb[0:1, 0:512],
                      start=True, stop=True)
            i = TE.matmul(pb[1][:, 0:512], or_sb[:], bco_sb[0:1, 0:512],
                          start=True, stop=True)
            mte = cTE.inc(i)
            wait(VE, cTE, mte)
            VE.drain()
            for dt_ in range(NDT):
                VE.tensor_tensor(ysb[:, dt_ * 512:(dt_ + 1) * 512],
                                 resid[:, rb + dt_ * 512:rb + dt_ * 512 + 512],
                                 pb[0][:, 0:512], ALU.mult)
            VE.drain()
            if with_router:
                for dt_ in range(NDT):
                    i = VE.tensor_tensor(ysb[:, dt_ * 512:(dt_ + 1) * 512],
                                         pb[1][:, 0:512],
                                         ysb[:, dt_ * 512:(dt_ + 1) * 512], ALU.add)
                mve = cVE.inc(i)
                read_bank(0, cVE, mve); read_bank(1, cVE, mve)
                wait(SE, cVE, mve)
                SE.drain()
                for dt_ in range(NDT):
                    i = SE.activation(
                        xnbf[:, xs(c) + dt_ * 512:xs(c) + dt_ * 512 + 512],
                        ysb[:, dt_ * 512:(dt_ + 1) * 512], AF.Identity)
                cSE.inc(i)
                emit_router(c, mve)
            else:
                for dt_ in range(NDT):
                    i = VE.tensor_tensor(
                        xnbf[:, xs(c) + dt_ * 512:xs(c) + dt_ * 512 + 512],
                        ysb[:, dt_ * 512:(dt_ + 1) * 512],
                        pb[1][:, 0:512], ALU.add)
                mve = cVE.inc(i)
                read_bank(0, cVE, mve); read_bank(1, cVE, mve)

        def emit_router(c, m_xn):
            te_take(0)
            wait(TE, cVE, m_xn)
            for tt in range(4):
                for kt in range(NDT):
                    i = TE.matmul(pb[0][:, tt * 8:tt * 8 + 8],
                                  ysb[:, kt * 512 + tt * 128:kt * 512 + tt * 128 + 128],
                                  gate_sb[:, kt * 8:kt * 8 + 8],
                                  start=(kt == 0), stop=(kt == NDT - 1))
            mte = cTE.inc(i)
            wait(VE, cTE, mte)
            VE.drain()
            for tt in range(4):
                VE.reduce_max(rm_sb[:, tt:tt + 1], pb[0][:, tt * 8:tt * 8 + 8], axis=AX)
            VE.drain()
            for tt in range(4):
                i = VE.tensor_scalar(rm_sb[:, 4 + tt:5 + tt], rm_sb[:, tt:tt + 1],
                                     -1.0, None, ALU.mult)
            mve = cVE.inc(i)
            wait(SE, cVE, mve)
            SE.drain()
            for tt in range(4):
                i = SE.activation(rp_sb[:, tt * 8:tt * 8 + 8],
                                  pb[0][:, tt * 8:tt * 8 + 8],
                                  AF.Exp, bias=rm_sb[:, 4 + tt:5 + tt])
            mse = cSE.inc(i)
            wait(VE, cSE, mse)
            VE.drain()
            for tt in range(4):
                VE.reduce_sum(rm_sb[:, 8 + tt:9 + tt], rp_sb[:, tt * 8:tt * 8 + 8],
                              axis=AX)
            VE.drain()
            for tt in range(4):
                VE.reciprocal(rm_sb[:, 8 + tt:9 + tt], rm_sb[:, 8 + tt:9 + tt])
            VE.drain()
            for tt in range(4):
                VE.tensor_scalar(rm_sb[:, 8 + tt:9 + tt], rm_sb[:, 8 + tt:9 + tt],
                                 0.5, None, ALU.mult)
            VE.drain()
            for tt in range(4):
                i = VE.tensor_scalar(mw_sb[:, tt * 8:tt * 8 + 8],
                                     pb[0][:, tt * 8:tt * 8 + 8],
                                     rm_sb[:, tt:tt + 1], rm_sb[:, 8 + tt:9 + tt],
                                     ALU.is_equal, ALU.mult)
            mve = cVE.inc(i)
            read_bank(0, cVE, mve)
            te_take(1)
            wait(TE, cVE, mve)
            for tt in range(4):
                i = TE.transpose(pb[1][0:8, tt * 128:(tt + 1) * 128],
                                 mw_sb[:, tt * 8:tt * 8 + 8], id_sb[:])
            mte = cTE.inc(i)
            wait(SE, cTE, mte)
            SE.drain()
            i = SE.activation(mwt_sb[0:8, 0:512], pb[1][0:8, 0:512], AF.Identity)
            mse = cSE.inc(i)
            read_bank(1, cSE, mse)

        def emit_ffn(c, pump):
            rb = rs(c)
            xb = xs(c)
            m_mwt = cSE.n
            for j in range(NDT):
                bi = j % 2
                te_take(bi)
                wait(TE, cSE, m_mwt)
                i = TE.matmul(pb[bi][:, 0:512], b2_sb[0:8, j * 128:(j + 1) * 128],
                              mwt_sb[0:8, 0:512], start=True, stop=True)
                mte = cTE.inc(i)
                wait(VE, cTE, mte)
                if j < 2:
                    VE.drain()
                r = resid[:, rb + j * 512:rb + j * 512 + 512]
                i = VE.tensor_tensor(r, pb[bi][:, 0:512], r, ALU.add)
                mve = cVE.inc(i)
                read_bank(bi, cVE, mve)
            for e in range(NE):
                bi = 4 + e % 2
                te_take(bi)
                i = TE.matmul(pb[bi][:, 0:512], sel_sb[0:8, e * 128:(e + 1) * 128],
                              mwt_sb[0:8, 0:512], start=True, stop=True)
                m_mbc = cTE.inc(i)
                for p8 in range(8):
                    p1 = load_piece(s_w1[e, p8], 8, 512)
                    w1v = piece_view(p1, 8, 512)
                    wait_piece(TE, p1)
                    hslot = (p8 % 2) * 2048
                    for j in range(4):
                        hb = st["hy"] % 2; st["hy"] += 1
                        te_take(hb)
                        for kt in range(8):
                            i = TE.matmul(pb[hb][:, 0:512],
                                          w1v[:, kt, j * 128:(j + 1) * 128],
                                          xnbf[:, xb + kt * 512:xb + kt * 512 + 512],
                                          start=(kt == 0), stop=(kt == 7))
                        mte = cTE.inc(i)
                        if j == 3:
                            cWF.inc(TE.drain())
                        wait(SE, cTE, mte)
                        if j == 0:
                            SE.drain()
                        bcol = e * 32 + p8 * 4 + j
                        i = SE.activation(hqb[:, hslot + j * 512:hslot + j * 512 + 512],
                                          pb[hb][:, 0:512], AF.Gelu,
                                          bias=b1_sb[:, bcol:bcol + 1])
                        mse = cSE.inc(i)
                        read_bank(hb, cSE, mse)
                    m_h = cSE.n
                    pump(3)
                    p2 = load_piece(s_w2[e, p8], 4, 1024)
                    w2v = piece_view(p2, 4, 1024)
                    wait_piece(TE, p2)
                    wait(TE, cSE, m_h)
                    for j2 in range(NDT):
                        yb = st["hy"] % 2; st["hy"] += 1
                        te_take(yb)
                        for kt in range(4):
                            i = TE.matmul(pb[yb][:, 0:512],
                                          w2v[:, kt, j2 * 128:(j2 + 1) * 128],
                                          hqb[:, hslot + kt * 512:hslot + kt * 512 + 512],
                                          start=(kt == 0), stop=(kt == 3))
                        mte = cTE.inc(i)
                        if j2 == NDT - 1:
                            cWF.inc(TE.drain())
                        wait(VE, cTE, mte)
                        if j2 == 0:
                            VE.drain()
                        ys = ysb[:, j2 * 512:(j2 + 1) * 512]
                        if p8 == 0:
                            i = VE.tensor_copy(ys, pb[yb][:, 0:512])
                        else:
                            i = VE.tensor_tensor(ys, pb[yb][:, 0:512], ys, ALU.add)
                        mve = cVE.inc(i)
                        read_bank(yb, cVE, mve)
                    pump(3)
                wait(VE, cTE, m_mbc)
                VE.drain()
                for j in range(NDT):
                    sl = (j % 2) * 512
                    wait(VE, cGP, st["mt_slot"][j % 2])
                    VE.drain()
                    i = VE.tensor_tensor(mt_sb[:, sl:sl + 512],
                                         ysb[:, j * 512:(j + 1) * 512],
                                         pb[bi][:, 0:512], ALU.mult)
                    mve = cVE.inc(i)
                    wait(GP, cVE, mve)
                    r = resid[:, rb + j * 512:rb + j * 512 + 512]
                    i = GP.tensor_tensor(r, mt_sb[:, sl:sl + 512], r, ALU.add)
                    st["mt_slot"][j % 2] = cGP.inc(i)
                read_bank(bi, cVE, cVE.n)

        def xrw(g, a, b):
            return xr_t[:, g * XRS + a:g * XRS + b]
        def yaw(g):
            return ya_t[:, g * 256:(g + 1) * 256]
        def st1s(t):
            return st1_sb[:, (t % 3) * 256:(t % 3) * 256 + 256]

        def p67_take():
            bi = 6 + st["p67"] % 2; st["p67"] += 1
            te_take(bi)
            return bi

        def mamba_gen(c, m_xn_ve):
            rb = rs(c)
            xb = xs(c)
            for w in range(2):
                import os as _os
                if _os.environ.get("EARLY_STOP") == "win1" and w == 1:
                    return
                t0 = w * 256
                xz_ms = {}
                m_bprev = st.get("m_scanb")
                for pc in range(4):
                    p = load_piece(s_win[pc], 8, 512)
                    wv = piece_view(p, 8, 512)
                    wait_piece(TE, p)
                    wait(TE, cVE, m_xn_ve)
                    for mt in range(4):
                        bi = p67_take()
                        for kt in range(8):
                            i = TE.matmul(
                                pb[bi][:, 0:256], wv[:, kt, mt * 128:(mt + 1) * 128],
                                xnbf[:, xb + kt * 512 + t0:xb + kt * 512 + t0 + 256],
                                start=(kt == 0), stop=(kt == 7))
                        mte = cTE.inc(i)
                        if mt == 3:
                            cWF.inc(TE.drain())
                        wait(SE, cTE, mte)
                        if pc == 0 and mt == 0:
                            wait(SE, cVE, m_bprev)
                            SE.drain()
                        gi = pc * 4 + mt
                        i = SE.activation(xrw(gi, 3, 259), pb[bi][:, 0:256],
                                          AF.Identity)
                        xz_ms[gi] = cSE.inc(i)
                        read_bank(bi, cSE, xz_ms[gi])
                    yield
                wait(VE, cSE, xz_ms[NG - 1])
                VE.drain()
                xr_g = xr_t[:, 0:NG * XRS].rearrange("p (g a) -> p g a", g=NG, a=XRS)
                halo_g = halo_sb[:].rearrange("p (g a) -> p g a", g=NG, a=3)
                VE.tensor_copy(xr_g[:, :, 0:3], halo_g)
                VE.drain()
                VE.tensor_copy(halo_g, xr_g[:, :, 256:259])
                VE.drain()
                silu_ms = {}
                for g0 in range(0, NG, 2):
                    if g0 >= 2:
                        wait(VE, cSE, silu_ms[g0 - 1])
                        VE.drain()
                    for g in (g0, g0 + 1):
                        sl = (g % 2) * 256
                        VE.tensor_scalar(cv_sb[:, sl:sl + 256], xrw(g, 0, 256),
                                         cw_sb[:, g * 4:g * 4 + 1], None, ALU.mult)
                    for k in range(1, 4):
                        VE.drain()
                        for g in (g0, g0 + 1):
                            sl = (g % 2) * 256
                            i = VE.scalar_tensor_tensor(
                                cv_sb[:, sl:sl + 256], xrw(g, k, 256 + k),
                                cw_sb[:, g * 4 + k:g * 4 + k + 1],
                                cv_sb[:, sl:sl + 256], ALU.mult, ALU.add)
                    mve = cVE.inc(i)
                    wait(SE, cVE, mve)
                    if g0 == 0:
                        SE.drain()
                    for g in (g0, g0 + 1):
                        sl = (g % 2) * 256
                        i = SE.activation(xrw(g, 3, 259), cv_sb[:, sl:sl + 256],
                                          AF.Silu, bias=cb_sb[:, g:g + 1])
                        silu_ms[g] = cSE.inc(i)
                    if g0 % 4 == 2:
                        yield
                SE.drain()
                for g in range(NG):
                    i = SE.activation(ybf_t[:, g * 256:(g + 1) * 256],
                                      xrw(g, 3, 259), AF.Identity)
                m_xbf = cSE.inc(i)
                yield
                te_take(6); te_take(7)
                wait(TE, cSE, m_xbf)
                for kt in range(NG):
                    TE.matmul(pb[6][0:64, 0:256], wx_sb[:, kt * 96:kt * 96 + 64],
                              ybf_t[:, kt * 256:(kt + 1) * 256],
                              start=(kt == 0), stop=(kt == NG - 1))
                for kt in range(NG):
                    TE.matmul(pb[7][0:16, 0:256], wx_sb[:, kt * 96 + 64:kt * 96 + 80],
                              ybf_t[:, kt * 256:(kt + 1) * 256],
                              start=(kt == 0), stop=(kt == NG - 1))
                for kt in range(NG):
                    i = TE.matmul(pb[7][0:16, 256:512],
                                  wx_sb[:, kt * 96 + 80:kt * 96 + 96],
                                  ybf_t[:, kt * 256:(kt + 1) * 256],
                                  start=(kt == 0), stop=(kt == NG - 1))
                mte = cTE.inc(i)
                wait(SE, cTE, mte)
                SE.drain()
                SE.activation(dl_sb[:], pb[6][0:64, 0:256], AF.Identity)
                i = SE.activation(bc_sb[0:16, 0:512], pb[7][0:16, 0:512], AF.Identity)
                m_bcs = cSE.inc(i)
                read_bank(6, cSE, m_bcs); read_bank(7, cSE, m_bcs)
                yield
                dtv_ms = {}
                for mt0 in range(0, NG, 2):
                    bis = []
                    for mt in (mt0, mt0 + 1):
                        bi = p67_take()
                        wait(TE, cSE, m_bcs)
                        i = TE.matmul(pb[bi][0:128, 0:256],
                                      wdt_sb[:, mt * 128:(mt + 1) * 128],
                                      dl_sb[:], start=True, stop=True)
                        bis.append(bi)
                    mte = cTE.inc(i)
                    wait(SE, cTE, mte)
                    if mt0 >= 2:
                        wait(SE, cVE, dtv_ms[mt0 - 1])
                    SE.drain()
                    for k, mt in enumerate((mt0, mt0 + 1)):
                        SE.activation(sp_sb[:, k * 256:k * 256 + 256],
                                      pb[bis[k]][:, 0:256], AF.Abs,
                                      bias=bdt_sb[:, mt:mt + 1])
                    SE.drain()
                    for k in range(2):
                        SE.activation(sp_sb[:, 512 + k * 256:512 + k * 256 + 256],
                                      sp_sb[:, k * 256:k * 256 + 256],
                                      AF.Exp, scale=-1.0)
                    SE.drain()
                    for k in range(2):
                        SE.activation(sp_sb[:, k * 256:k * 256 + 256],
                                      sp_sb[:, 512 + k * 256:512 + k * 256 + 256],
                                      AF.Ln, bias=1.0)
                    SE.drain()
                    for k, mt in enumerate((mt0, mt0 + 1)):
                        i = SE.activation(dt_t[:, mt * 256:(mt + 1) * 256],
                                          pb[bis[k]][:, 0:256],
                                          AF.Relu, bias=bdt_sb[:, mt:mt + 1])
                    mse = cSE.inc(i)
                    read_bank(bis[0], cSE, mse); read_bank(bis[1], cSE, mse)
                    wait(VE, cSE, mse)
                    VE.drain()
                    for k, mt in enumerate((mt0, mt0 + 1)):
                        i = VE.tensor_tensor(dt_t[:, mt * 256:(mt + 1) * 256],
                                             sp_sb[:, k * 256:k * 256 + 256],
                                             dt_t[:, mt * 256:(mt + 1) * 256],
                                             ALU.add)
                        dtv_ms[mt] = cVE.inc(i)
                    if mt0 % 4 == 2:
                        yield
                VE.drain()
                for g0 in range(0, NG, 2):
                    for g in (g0, g0 + 1):
                        VE.tensor_scalar(yaw(g), xrw(g, 3, 259), D_sb[:, g:g + 1],
                                         None, ALU.mult)
                    VE.drain()
                    for g in (g0, g0 + 1):
                        i = VE.tensor_tensor(xrw(g, 3, 259),
                                             dt_t[:, g * 256:(g + 1) * 256],
                                             xrw(g, 3, 259), ALU.mult)
                m_u = cVE.inc(i)
                VE.drain()
                yield
                base_gp = cGP.n
                scan_ms = {}
                gp_ms = {}
                pend = []
                if True:
                    GP.drain()
                a_ms = {}
                def emit_a(t):
                    s_, g_ = t // NG, t % NG
                    wait(SE, cVE, dtv_ms[g_])
                    if t >= 3:
                        wait(SE, cVE, scan_ms[t - 3])
                    SE.drain()
                    ii = SE.activation(sa_sb[:, (t % 3) * 256:(t % 3) * 256 + 256],
                                       dt_t[:, g_ * 256:(g_ + 1) * 256], AF.Exp,
                                       scale=A_sb[:, g_ * 16 + s_:g_ * 16 + s_ + 1])
                    a_ms[t] = cSE.inc(ii)

                for s in range(DS):
                    bB = 6 + (s % 2)
                    te_take(bB)
                    TE.matmul(pb[bB][:, 0:256], sel_sb[0:16, s * 128:(s + 1) * 128],
                              bc_sb[0:16, 0:256], start=True, stop=True)
                    i = TE.matmul(pb[bB][:, 256:512],
                                  sel_sb[0:16, s * 128:(s + 1) * 128],
                                  bc_sb[0:16, 256:512], start=True, stop=True)
                    mte = cTE.inc(i)
                    csl = (s % 2) * 512
                    if s >= 2:
                        wait(SE, cGP, gp_ms[(s - 2) * NG + (NG - 1)])
                    wait(SE, cTE, mte)
                    SE.drain()
                    i = SE.activation(cc_sb[:, csl:csl + 512], pb[bB][:, 0:512],
                                      AF.Identity)
                    msec = cSE.inc(i)
                    read_bank(bB, cSE, msec)
                    for g in range(NG):
                        t = s * NG + g
                        if t == 0:
                            emit_a(0)
                        if t + 1 < DS * NG:
                            emit_a(t + 1)
                        wait(VE, cSE, a_ms[t])
                        wait(VE, cTE, mte)
                        wait(VE, cSE, msec)
                        ib = VE.tensor_tensor(sb1_sb[:, (t % 2) * 256:(t % 2) * 256 + 256],
                                              xrw(g, 3, 259), cc_sb[:, csl:csl + 256],
                                              ALU.mult)
                        mvb = cVE.inc(ib)
                        for lag in list(pend):
                            if lag[0] <= t - 2:
                                pend.remove(lag)
                                wait(VE, cGP, gp_ms[lag[0]])
                                i2 = VE.tensor_tensor(yaw(lag[1]), st1s(lag[0]),
                                                      yaw(lag[1]), ALU.add)
                                cVE.inc(i2)
                        wait(VE, cVE, mvb)
                        hsl = (g % 2) * 256
                        if t >= 2:
                            wait(VE, cGP, gp_ms[t - 2])
                        i = VE.tensor_tensor_scan(
                            sh_sb[:, hsl:hsl + 256],
                            sa_sb[:, (t % 3) * 256:(t % 3) * 256 + 256],
                            sb1_sb[:, (t % 2) * 256:(t % 2) * 256 + 256],
                            hst_sb[:, g * 16 + s:g * 16 + s + 1],
                            ALU.mult, ALU.add)
                        scan_ms[t] = cVE.inc(i)
                        wait(GP, cVE, scan_ms[t])
                        wait(GP, cSE, msec)
                        GP.tensor_tensor(st1s(t), sh_sb[:, hsl:hsl + 256],
                                         cc_sb[:, csl + 256:csl + 512], ALU.mult)
                        i = GP.tensor_copy(hst_sb[:, g * 16 + s:g * 16 + s + 1],
                                           sh_sb[:, hsl + 255:hsl + 256])
                        gp_ms[t] = cGP.inc(i)
                        pend.append((t, g))
                        if g % 4 == 3:
                            yield
                    read_bank(bB, cVE, cVE.n)
                st["m_scanb"] = cVE.n
                for lag in pend:
                    wait(VE, cGP, gp_ms[lag[0]])
                    i = VE.tensor_tensor(yaw(lag[1]), st1s(lag[0]),
                                         yaw(lag[1]), ALU.add)
                    cVE.inc(i)
                m_scan = cVE.n
                import os as _os
                if _os.environ.get("EARLY_STOP") == "scan0" and w == 0:
                    return
                yield
                gt_ms = {}
                for pc in range(4):
                    p = load_piece(s_win[4 + pc], 8, 512)
                    wv = piece_view(p, 8, 512)
                    wait_piece(TE, p)
                    for mt in range(4):
                        bi = p67_take()
                        for kt in range(8):
                            i = TE.matmul(
                                pb[bi][:, 0:256], wv[:, kt, mt * 128:(mt + 1) * 128],
                                xnbf[:, xb + kt * 512 + t0:xb + kt * 512 + t0 + 256],
                                start=(kt == 0), stop=(kt == 7))
                        mte = cTE.inc(i)
                        if mt == 3:
                            cWF.inc(TE.drain())
                        gi = pc * 4 + mt
                        wait(SE, cTE, mte)
                        if gi >= 2:
                            wait(SE, cVE, gt_ms[gi - 2])
                        SE.drain()
                        sl = (gi % 2) * 256
                        i = SE.activation(gz_sb[:, sl:sl + 256], pb[bi][:, 0:256],
                                          AF.Silu)
                        mse = cSE.inc(i)
                        read_bank(bi, cSE, mse)
                        wait(VE, cSE, mse)
                        if gi == 0:
                            wait(VE, cVE, m_scan)
                            VE.drain()
                        i = VE.tensor_tensor(ybf_t[:, gi * 256:(gi + 1) * 256],
                                             yaw(gi), gz_sb[:, sl:sl + 256],
                                             ALU.mult)
                        gt_ms[gi] = cVE.inc(i)
                    yield
                m_ybf = cVE.n
                for pc in range(4):
                    p = load_piece(s_wout[pc], 16, 256)
                    wv = piece_view(p, 16, 256)
                    wait_piece(TE, p)
                    wait(TE, cVE, m_ybf)
                    for mt in range(2):
                        bi = p67_take()
                        for kt in range(NG):
                            i = TE.matmul(pb[bi][:, 0:256],
                                          wv[:, kt, mt * 128:(mt + 1) * 128],
                                          ybf_t[:, kt * 256:(kt + 1) * 256],
                                          start=(kt == 0), stop=(kt == NG - 1))
                        mte = cTE.inc(i)
                        if mt == 1:
                            cWF.inc(TE.drain())
                        wait(VE, cTE, mte)
                        dt_i = pc * 2 + mt
                        if dt_i == 0:
                            VE.drain()
                        r = resid[:, rb + dt_i * 512 + t0:rb + dt_i * 512 + t0 + 256]
                        i = VE.tensor_tensor(r, pb[bi][:, 0:256], r, ALU.add)
                        mve = cVE.inc(i)
                        read_bank(bi, cVE, mve)
                    if pc % 2 == 1:
                        yield

        def emit_in(c):
            wait(SY, semO[0], semO[0].n)
            wait(SY, semO[1], semO[1].n)
            wait(SY, cSE, cSE.n)
            wait(SY, cTE, cTE.n)
            rb = rs(c)
            for tt in range(4):
                g = c * 4 + tt
                wait(SY, cSE, st["stage_in"][tt % 2])
                si = semI[g % 2]
                SY.dma_start(hq[:, (tt % 2) * 1024:(tt % 2) * 1024 + 1024],
                             d_x[g * 128:(g + 1) * 128, :]).then_inc(si.h, 16)
                si.n += 16
                wait(TE, si, si.n)
                for dt_ in range(NDT):
                    bi = dt_ % 2
                    te_take(bi)
                    i = TE.transpose(pb[bi][:, 0:128],
                                     hq[:, (tt % 2) * 1024 + dt_ * 128:
                                         (tt % 2) * 1024 + dt_ * 128 + 128],
                                     id_sb[:])
                    mte = cTE.inc(i)
                    wait(SE, cTE, mte)
                    if dt_ == 0:
                        SE.drain()
                    i = SE.activation(
                        resid[:, rb + dt_ * 512 + tt * 128:rb + dt_ * 512 + tt * 128 + 128],
                        pb[bi][:, 0:128], AF.Identity)
                    mse = cSE.inc(i)
                    read_bank(bi, cSE, mse)
                st["stage_in"][tt % 2] = mse

        def emit_out(c):
            rb = rs(c)
            wait(TE, cGP, cGP.n)
            wait(SE, cGP, cGP.n)
            for tt in range(4):
                g = c * 4 + tt
                so = semO[g % 2]
                for dt_ in range(NDT):
                    bi = dt_ % 2
                    te_take(bi)
                    i = TE.transpose(
                        pb[bi][:, 0:128],
                        resid[:, rb + dt_ * 512 + tt * 128:rb + dt_ * 512 + tt * 128 + 128],
                        id_sb[:])
                    mte = cTE.inc(i)
                    wait(SE, cTE, mte)
                    if dt_ == 0:
                        wait(SE, so, so.n)
                        SE.drain()
                    i = SE.activation(hq[:, (tt % 2) * 1024 + dt_ * 128:
                                          (tt % 2) * 1024 + dt_ * 128 + 128],
                                      pb[bi][:, 0:128], AF.Identity)
                    mse = cSE.inc(i)
                    read_bank(bi, cSE, mse)
                wait(SY, cSE, mse)
                SY.dma_start(d_out[g * 128:(g + 1) * 128, :],
                             hq[:, (tt % 2) * 1024:(tt % 2) * 1024 + 1024]
                             ).then_inc(so.h, 16)
                so.n += 16

        st["gen"] = None
        def pump(k):
            g = st["gen"]
            if g is None:
                return
            for _ in range(k):
                try:
                    next(g)
                except StopIteration:
                    st["gen"] = None
                    return
        def flush():
            g = st["gen"]
            if g is not None:
                for _ in g:
                    pass
                st["gen"] = None

        for c in range(nch):
            emit_in(c)
            emit_ln(c, True)
            emit_ffn(c, pump)
            emit_ln(c, False)
            flush()
            st["gen"] = mamba_gen(c, cVE.n)
            if c >= 1:
                j = c - 1
                emit_ln(j, True)
                emit_ffn(j, pump)
                emit_out(j)
        flush()
        emit_ln(nch - 1, True)
        emit_ffn(nch - 1, lambda k: None)
        emit_out(nch - 1)
        SY.wait_ge(semO[0].h, semO[0].n)
        SY.wait_ge(semO[1].h, semO[1].n)

    return nc


def _prearrange(inputs):
    w1 = inputs["L0_ffn_w1"]
    w2 = inputs["L0_ffn_w2"]
    win = inputs["L0_attn_W_in"]
    wout = inputs["L0_attn_W_out"]
    wx = inputs["L0_attn_W_x"]
    wdt = inputs["L0_attn_W_dt"]
    gate = inputs["L0_ffn_gate_w"]
    b1 = inputs["L0_ffn_b1"]
    b2 = inputs["L0_ffn_b2"]
    cw = inputs["L0_attn_conv_w"]
    cb = inputs["L0_attn_conv_b"]
    bdt = inputs["L0_attn_b_dt"]
    Dv = inputs["L0_attn_D"]
    alog = inputs["L0_attn_A_log"]

    d = {}
    d["w1s"] = np.ascontiguousarray(
        w1.reshape(8, 8, 128, 8, 512).transpose(0, 3, 1, 2, 4))
    d["w2s"] = np.ascontiguousarray(w2.reshape(8, 8, 4, 128, 1024))
    d["wins"] = np.ascontiguousarray(
        win.reshape(8, 128, 8, 512).transpose(2, 0, 1, 3))
    d["wouts"] = np.ascontiguousarray(
        wout.reshape(16, 128, 4, 256).transpose(2, 0, 1, 3))
    wxdt = np.zeros((128, 3584), dtype=np.float32)
    wxdt[:, 0:1536] = wx.reshape(16, 128, 96).transpose(1, 0, 2).reshape(128, 1536)
    wxdt[0:64, 1536:3584] = wdt
    d["wxdt"] = wxdt
    d["gates"] = np.ascontiguousarray(
        gate.reshape(8, 128, 8).transpose(1, 0, 2).reshape(128, 64))
    d["b2s"] = np.ascontiguousarray(b2)
    d["b1s"] = np.ascontiguousarray(
        b1.reshape(8, 8, 4, 128).transpose(3, 0, 1, 2).reshape(128, 256))
    d["convw"] = np.ascontiguousarray(
        cw.T.reshape(16, 128, 4).transpose(1, 0, 2).reshape(128, 64))
    d["convb"] = np.ascontiguousarray(cb.reshape(16, 128).T)
    d["bdts"] = np.ascontiguousarray(bdt.reshape(16, 128).T)
    d["Dg"] = np.ascontiguousarray(Dv.reshape(16, 128).T)
    d["alog"] = np.ascontiguousarray(
        alog.reshape(16, 128, 16).transpose(1, 0, 2).reshape(128, 256))
    d["ident"] = np.eye(128, dtype=np.float32)
    d["onesc"] = np.ones((128, 1), dtype=np.float32)
    d["onesr"] = np.ones((1, 128), dtype=np.float32)
    sel = np.zeros((16, 2048), dtype=np.float32)
    for s in range(16):
        sel[s, s * 128:(s + 1) * 128] = 1.0
    d["sel16"] = sel
    return d


def kernel(**inputs):
    x = inputs["L0_attn_hidden_states"]
    shared = _prearrange(inputs)
    nc = build_nc()
    in_maps = []
    for core in range(8):
        m = dict(shared)
        m["x"] = np.ascontiguousarray(x[core])
        in_maps.append(m)
    res = run_bass_kernel_spmd(nc, in_maps, core_ids=list(range(8)))
    out = np.stack([np.asarray(res.results[i]["out"]) for i in range(8)], axis=0)
    return out.astype(np.float32, copy=False)
